# version 31
# baseline (speedup 1.0000x reference)
import contextlib
import numpy as np
import ml_dtypes
import concourse.bass as bass
import concourse.mybir as mybir
from concourse.bass_utils import run_bass_kernel_spmd

F32 = mybir.dt.float32
BF16 = mybir.dt.bfloat16
AO = mybir.AluOpType
AF = mybir.ActivationFunctionType
AX = mybir.AxisListType.X

MAGIC = 12582912.0
SQRT_QKD = float(128) ** 0.5
T_TILES = 16
N_GROUPS = 8
DIM = 1024
HID = 2048
QKD = 128
H_FFN = 4096



class _HazardGuard:
    _PASS = ("wait_ge", "nop", "drain", "dma_start")

    def __init__(self, hazard=True):
        self._e = None
        self._hazard = hazard
        self._last = None
        self._hist = []

    def bind(self, eng):
        self._e = eng
        return self

    @staticmethod
    def _names(xs):
        s = []
        for x in xs:
            t = getattr(x, "tensor", None)
            if t is None:
                continue
            lo, hi = 0, 1 << 30
            try:
                fsz = x.free_size()
                fnb = x.free_nbytes()
                if x.ndim == 2 and x.stride() == 1 and fsz > 0:
                    esz = fnb // fsz
                    lo = x.offset * esz
                    hi = lo + fnb
            except Exception:
                pass
            s.append((t.name, lo, hi))
        return s

    @staticmethod
    def _clash(a, b):
        for na, la, ha in a:
            for nb, lb, hb in b:
                if na == nb and la < hb and lb < ha:
                    return True
        return False

    def _guarded(self, outs, ins, call):
        if self._hazard:
            w = self._names(outs)
            r = self._names(ins)
            for pw, pr in self._hist:
                if (self._clash(r, pw) or self._clash(w, pw)
                        or self._clash(w, pr)):
                    self._e.drain()
                    self._hist = []
                    break
            res = call()
            self._hist = (self._hist + [(w, r)])[-2:]
        else:
            res = call()
        self._last = res
        return res

    def mark(self, sem, k=1):
        sem.inc(self._last, k)

    def matmul(self, out, lhsT, rhs, **kw):
        return self._guarded([out], [lhsT, rhs],
                             lambda: self._e.matmul(out, lhsT, rhs, **kw))

    def transpose(self, out, in_, ident):
        return self._guarded([out], [in_, ident],
                             lambda: self._e.transpose(out, in_, ident))

    def __getattr__(self, name):
        return getattr(self._e, name)

    def wait_ge(self, *a, **k):
        return self._e.wait_ge(*a, **k)

    def nop(self, *a, **k):
        return self._e.nop(*a, **k)

    def drain(self, *a, **k):
        self._hist = []
        return self._e.drain(*a, **k)

    def dma_start(self, *a, **k):
        return self._e.dma_start(*a, **k)

    def memset(self, ap, c):
        return self._guarded([ap], [], lambda: self._e.memset(ap, c))

    def tensor_copy(self, out, in_):
        return self._guarded([out], [in_], lambda: self._e.tensor_copy(out, in_))

    def reciprocal(self, out, in_):
        return self._guarded([out], [in_], lambda: self._e.reciprocal(out, in_))

    def tensor_tensor(self, out, a, b, op):
        return self._guarded([out], [a, b],
                             lambda: self._e.tensor_tensor(out, a, b, op))

    def tensor_scalar(self, out, in0, s1, s2, op0, op1=None, accum_out=None):
        outs = [out] + ([accum_out] if accum_out is not None else [])
        kw = {}
        if op1 is not None:
            kw["op1"] = op1
        if accum_out is not None:
            kw["accum_out"] = accum_out
        return self._guarded(outs, [in0, s1, s2],
                             lambda: self._e.tensor_scalar(out, in0, s1, s2,
                                                           op0, **kw))

    def scalar_tensor_tensor(self, out, in0, scalar, in1, op0, op1,
                             accum_out=None):
        outs = [out] + ([accum_out] if accum_out is not None else [])
        kw = {"accum_out": accum_out} if accum_out is not None else {}
        return self._guarded(outs, [in0, scalar, in1],
                             lambda: self._e.scalar_tensor_tensor(
                                 out, in0, scalar, in1, op0, op1, **kw))

    def tensor_reduce(self, out, in_, axis, op, apply_absolute_value=None):
        kw = {}
        if apply_absolute_value is not None:
            kw["apply_absolute_value"] = apply_absolute_value
        return self._guarded([out], [in_],
                             lambda: self._e.tensor_reduce(out, in_, axis=axis,
                                                           op=op, **kw))

    def activation(self, out, in_, func, bias=None, scale=None, accum_out=None):
        outs = [out] + ([accum_out] if accum_out is not None else [])
        kw = {}
        if bias is not None:
            kw["bias"] = bias
        if scale is not None:
            kw["scale"] = scale
        if accum_out is not None:
            kw["accum_out"] = accum_out
        return self._guarded(outs, [in_, bias, scale],
                             lambda: self._e.activation(out, in_, func, **kw))


class SemT:
    def __init__(self, ctx, nc, name):
        self.h = ctx.enter_context(nc.semaphore(name))
        self.n = 0

    def inc(self, instr, k=1):
        instr.then_inc(self.h, k)
        self.n += k
        return self.n


def build(debug=False):
    nc = bass.Bass("TRN2", target_bir_lowering=False,
                   detect_race_conditions=False)

    d_x = nc.dram_tensor("x", [2048, DIM], F32, kind="ExternalInput").ap()
    d_Wh = nc.dram_tensor("Wh", [DIM, 2 * HID], F32, kind="ExternalInput").ap()
    d_Wqk = nc.dram_tensor("Wqk", [DIM, QKD], F32, kind="ExternalInput").ap()
    d_Wo = nc.dram_tensor("Wo", [HID, DIM], F32, kind="ExternalInput").ap()
    d_W1T = nc.dram_tensor("W1T", [DIM, H_FFN], F32, kind="ExternalInput").ap()
    d_W2T = nc.dram_tensor("W2T", [H_FFN, DIM], F32, kind="ExternalInput").ap()
    d_bh_v = nc.dram_tensor("bh_v", [1, HID], F32, kind="ExternalInput").ap()
    d_bhg_t = nc.dram_tensor("bhg_t", [128, 16], F32, kind="ExternalInput").ap()
    d_bo = nc.dram_tensor("bo_row", [1, DIM], F32, kind="ExternalInput").ap()
    d_b1 = nc.dram_tensor("b1_row", [1, H_FFN], F32, kind="ExternalInput").ap()
    d_b2 = nc.dram_tensor("b2_row", [1, DIM], F32, kind="ExternalInput").ap()
    d_l1g = nc.dram_tensor("ln1g_row", [1, DIM], F32, kind="ExternalInput").ap()
    d_l1b = nc.dram_tensor("ln1b_row", [1, DIM], F32, kind="ExternalInput").ap()
    d_l2g = nc.dram_tensor("ln2g_row", [1, H_FFN], F32, kind="ExternalInput").ap()
    d_l2b = nc.dram_tensor("ln2b_row", [1, H_FFN], F32, kind="ExternalInput").ap()
    d_bqk = nc.dram_tensor("bqk_col", [128, 1], F32, kind="ExternalInput").ap()
    d_gam = nc.dram_tensor("gammaT", [128, 4], F32, kind="ExternalInput").ap()
    d_bet = nc.dram_tensor("betaT", [128, 4], F32, kind="ExternalInput").ap()
    d_ngT = nc.dram_tensor("normgT", [128, 8], F32, kind="ExternalInput").ap()
    d_nbT = nc.dram_tensor("normbT", [128, 8], F32, kind="ExternalInput").ap()
    d_bias = nc.dram_tensor("biasTg", [128, 512], F32, kind="ExternalInput").ap()
    d_idf = nc.dram_tensor("identf", [128, 128], F32, kind="ExternalInput").ap()
    d_idb = nc.dram_tensor("identb", [128, 128], BF16, kind="ExternalInput").ap()
    d_onec = nc.dram_tensor("ones_col", [128, 1], F32, kind="ExternalInput").ap()
    d_oner = nc.dram_tensor("ones_row", [1, 128], F32, kind="ExternalInput").ap()

    d_out = nc.dram_tensor("out", [2048, DIM], F32, kind="ExternalOutput").ap()

    d_quadT = nc.dram_tensor("quadT_scr", [N_GROUPS, 16, 128, 256], F32,
                             kind="Internal").ap()
    d_gateT = nc.dram_tensor("gateT_scr", [N_GROUPS, 16, 128, 256], F32,
                             kind="Internal").ap()
    t_arin = nc.dram_tensor("ar_in", [128, HID], F32, kind="Internal")
    t_arout = nc.dram_tensor("ar_out", [128, HID], F32, kind="Internal")

    dbg = {}
    if debug:
        for nm, shp in [("dbg_NT", [128, 8 * 2048]), ("dbg_linkv", [128, HID]),
                        ("dbg_AT", [128, 4096]), ("dbg_linQ", [128, 2048]),
                        ("dbg_stats", [128, 64]), ("dbg_W1Q", [128, 4096]),
                        ("dbg_sw", [128, 4]), ("dbg_wsum", [128, 8]),
                        ("dbg_ffn", [2048, DIM])]:
            dbg[nm] = nc.dram_tensor(nm, shp, F32, kind="ExternalOutput").ap()

    ctx = contextlib.ExitStack()
    with ctx:
        def sb(name, shape, dt=F32, st=None, side=None):
            if side:
                return (st or ctx).enter_context(
                    nc.sbuf_tensor(name, shape, dt, side=side))
            return (st or ctx).enter_context(nc.sbuf_tensor(name, shape, dt))

        rsq0 = sb("rsq0", [128, T_TILES]); nm0 = sb("nm0", [128, T_TILES])
        rsq2 = sb("rsq2", [128, T_TILES]); nm2 = sb("nm2", [128, T_TILES])
        rsq02 = sb("rsq02", [128, T_TILES]); nm02 = sb("nm02", [128, T_TILES])
        gamT = sb("gamT", [128, 4]); betT = sb("betT", [128, 4])
        ngT = sb("ngT", [128, 8]); nbT = sb("nbT", [128, 8])
        bqk = sb("bqk", [128, 1]); onec = sb("onec", [128, 1])
        oner = sb("oner", [1, 128])
        idf = sb("idf", [128, 128]); idb = sb("idb", [128, 128], BF16)
        c127 = sb("c127", [128, 1])
        sw1c = sb("sw1c", [128, 1]); sw2c = sb("sw2c", [128, 1])
        mw1c = sb("mw1c", [128, 1]); mw2c = sb("mw2c", [128, 1])
        stat1 = sb("stat1", [128, 1]); stat2 = sb("stat2", [128, 1])
        stat3 = sb("stat3", [128, 1]); stat4 = sb("stat4", [128, 1])
        stat5 = sb("stat5", [128, 1]); eps_c = sb("eps_c", [128, 1])
        mh16 = sb("mh16", [128, T_TILES])
        lkv = sb("lkv", [128, HID])
        linQF = sb("linQF", [128, 2048])

        pb = [ctx.enter_context(nc.psum_tensor("pb%d" % i, [128, 512], F32))
              for i in range(7)]
        pbf = ctx.enter_context(nc.psum_tensor("pbf", [128, 1024], BF16))

        sems = {}
        for s in ("sld", "sgd", "sv", "ss", "st_", "scc", "sbar",
                  "pA", "pB", "pC", "pD", "saux", "sxc", "sgp",
                  "pE", "pF"):
            sems[s] = SemT(ctx, nc, s)
        sld, sgd, sv, ss, st_, scc, sbar = (sems[k] for k in
                                            ("sld", "sgd", "sv", "ss", "st_",
                                             "scc", "sbar"))
        pA, pB, saux, sxc, sgp = (sems[k] for k in
                                  ("pA", "pB", "saux", "sxc", "sgp"))
        par = [pA, pB]
        parb = [sems["pC"], sems["pD"]]
        pard = [sems["pE"], sems["pF"]]

        block = ctx.enter_context(nc.Block())
        guards = {"vector": _HazardGuard(), "scalar": _HazardGuard(),
                  "tensor": _HazardGuard(hazard=False),
                  "gpsimd": _HazardGuard()}

        def on(eng):
            def deco(fn):
                if eng in guards:
                    getattr(block, eng)(lambda e: fn(guards[eng].bind(e)))
                else:
                    getattr(block, eng)(fn)
            return deco

        def barrier():
            vals = [(sld, sld.n), (sgd, sgd.n), (ss, ss.n), (st_, st_.n),
                    (pA, pA.n), (pB, pB.n), (saux, saux.n), (sxc, sxc.n),
                    (sems["pC"], sems["pC"].n), (sems["pD"], sems["pD"].n)]

            @on("vector")
            def _(v):
                for s, n in vals:
                    v.wait_ge(s.h, n)
                sv.inc(v._e.nop() if isinstance(v, _HazardGuard) else v.nop())

            @on("gpsimd")
            def _(g):
                g.wait_ge(sv.h, sv.n)
                sbar.inc(g.drain())
            bar_n = sbar.n
            for eng in ("sync", "scalar", "tensor", "vector"):
                @on(eng)
                def _(e):
                    e.wait_ge(sbar.h, bar_n)

        @on("sync")
        def _(sy):
            for dst, src in [(gamT, d_gam), (betT, d_bet), (ngT, d_ngT),
                             (nbT, d_nbT), (bqk, d_bqk), (onec, d_onec),
                             (oner, d_oner), (idf, d_idf), (idb, d_idb)]:
                saux.inc(sy.dma_start(out=dst[:], in_=src), 16)
        n_const = saux.n

        @on("vector")
        def _(v):
            v.wait_ge(saux.h, n_const)
            v.memset(c127[:], 127.0)
            v.memset(eps_c[:], 1e-5)
            v.memset(mh16[:], -0.5)
            v.mark(sv)
        ph0_m = sv.n

        ffnw_st = contextlib.ExitStack()
        W1Q = sb("W1Q", [128, 8 * H_FFN], BF16, st=ffnw_st)
        W2QT = sb("W2QT", [128, 32 * DIM], BF16, st=ffnw_st)
        wst = contextlib.ExitStack()
        wbuf = [sb("wbuf%d" % i, [128, HID], st=wst) for i in range(2)]
        wsum = sb("wsum", [128, 8], st=wst)
        ybc_w = sb("ybc_w", [128, 2 * HID], BF16, st=wst)
        wscr = sb("wscr", [128, HID], st=wst)
        d0x = [sb("d0x%d" % i, [128, DIM], st=wst) for i in range(2)]
        d0a1 = sb("d0a1", [128, DIM], st=wst)
        d0a2 = sb("d0a2", [128, DIM], st=wst)
        d0sx = sb("d0sx", [128, T_TILES], st=wst)
        d0sxx = sb("d0sxx", [128, T_TILES], st=wst)
        d0tmp = sb("d0tmp", [128, T_TILES], st=wst)

        def quant_weight(jobs, dst, swc, mwc):
            nj = len(jobs)
            loads = []
            for j in range(min(2, nj)):
                @on("sync")
                def _(sy, j=j):
                    sy.wait_ge(sv.h, sv.n)
                    loads.append(par[j % 2].inc(sy.dma_start(
                        out=wbuf[j % 2][:, :jobs[j][2]], in_=jobs[j][0]), 16))
            for j in range(nj):
                ld_n = loads[j]
                ck = jobs[j][2]

                @on("vector")
                def _(v, j=j, ld_n=ld_n, ck=ck):
                    v.wait_ge(par[j % 2].h, ld_n)
                    col = j % 8
                    if j < 8:
                        v.tensor_reduce(wsum[:, col:col + 1],
                                        wbuf[j % 2][:, :ck], axis=AX,
                                        op=AO.add, apply_absolute_value=True)
                    else:
                        v.tensor_reduce(stat1[:], wbuf[j % 2][:, :ck],
                                        axis=AX, op=AO.add,
                                        apply_absolute_value=True)
                        v.tensor_tensor(wsum[:, col:col + 1],
                                        wsum[:, col:col + 1], stat1[:], AO.add)
                    v.mark(sv)
                if j + 2 < nj:
                    v_n = sv.n

                    @on("sync")
                    def _(sy, j=j, v_n=v_n):
                        sy.wait_ge(sv.h, v_n)
                        loads.append(par[j % 2].inc(sy.dma_start(
                            out=wbuf[j % 2][:, :jobs[j + 2][2]],
                            in_=jobs[j + 2][0]), 16))

            @on("vector")
            def _(v):
                v.tensor_reduce(stat1[:], wsum[:, :min(8, nj)], axis=AX,
                                op=AO.add)
                v.mark(sv)
            v_n = sv.n

            @on("tensor")
            def _(t):
                t.wait_ge(sv.h, v_n)
                t.matmul(pb[6][0:1, 0:1], onec[:], stat1[:], start=True,
                         stop=True)
                t.mark(st_)
            t_n = st_.n

            @on("vector")
            def _(v, t_n=t_n):
                v.wait_ge(st_.h, t_n)
                v.tensor_scalar(stat2[0:1, :], pb[6][0:1, 0:1],
                                1.0 / 4194304.0, 1e-5, AO.mult, AO.max)
                v.reciprocal(stat3[0:1, :], stat2[0:1, :])
                v.mark(sv)
            v_n = sv.n

            @on("tensor")
            def _(t, v_n=v_n):
                t.wait_ge(sv.h, v_n)
                t.matmul(pb[6][:, 1:2], oner[:], stat3[0:1, 0:1], start=True,
                         stop=True)
                t.matmul(pb[6][:, 2:3], oner[:], stat2[0:1, 0:1], start=True,
                         stop=True)
                t.mark(st_)
            t_n = st_.n

            @on("vector")
            def _(v, t_n=t_n):
                v.wait_ge(st_.h, t_n)
                v.tensor_copy(swc[:], pb[6][:, 1:2])
                v.tensor_copy(mwc[:], pb[6][:, 2:3])
                v.mark(sv)

            loads2 = []
            for j in range(min(2, nj)):
                @on("sync")
                def _(sy, j=j):
                    sy.wait_ge(sv.h, sv.n)
                    loads2.append(par[j % 2].inc(sy.dma_start(
                        out=wbuf[j % 2][:, :jobs[j][2]], in_=jobs[j][0]), 16))
            for j in range(nj):
                ld_n = loads2[j]
                ck = jobs[j][2]
                dcol = jobs[j][1]

                @on("vector")
                def _(v, j=j, ld_n=ld_n, ck=ck, dcol=dcol):
                    v.wait_ge(par[j % 2].h, ld_n)
                    yv = ybc_w[:].bitcast(F32)[:, :ck]
                    v.tensor_scalar(yv, wbuf[j % 2][:, :ck], swc[:],
                                    MAGIC, AO.mult, AO.add)
                    v.tensor_scalar(wscr[:, :ck], yv, MAGIC, -1.0,
                                    AO.subtract, AO.max)
                    v.tensor_scalar(dst[:, dcol:dcol + ck],
                                    wscr[:, :ck], 1.0, None, AO.min)
                    v.mark(sv)
                if j + 2 < nj:
                    v_n = sv.n

                    @on("sync")
                    def _(sy, j=j, v_n=v_n):
                        sy.wait_ge(sv.h, v_n)
                        loads2.append(par[j % 2].inc(sy.dma_start(
                            out=wbuf[j % 2][:, :jobs[j + 2][2]],
                            in_=jobs[j + 2][0]), 16))

        jobs1 = [(d_W1T[r * 128:(r + 1) * 128, hh * HID:(hh + 1) * HID],
                  (2 * r + hh) * HID, HID) for r in range(8) for hh in range(2)]
        jobs2 = [(d_W2T[r * 128:(r + 1) * 128, :], r * DIM, DIM)
                 for r in range(32)]
        def d0_pass(use_xn, ready_m=0):
            pard_base = [pard[0].n, pard[1].n]
            d0_sbase = ss.n
            for t in range(T_TILES):
                @on("gpsimd")
                def _(gp, t=t):
                    if use_xn and t == 0:
                        gp.wait_ge(ss.h, ss.n)
                    if t >= 2:
                        gp.wait_ge(ss.h, d0_sbase + t - 1)
                    pard[t % 2].inc(gp.dma_start(
                        out=d0x[t % 2][:],
                        in_=d_x[t * 128:(t + 1) * 128, :]), 16)

            @on("scalar")
            def _(sc, use_xn=use_xn, stats_m=ready_m):
                for t in range(T_TILES):
                    sc.wait_ge(pard[t % 2].h,
                               pard_base[t % 2] + 16 * (t // 2 + 1))
                    if use_xn and t == 0:
                        sc.wait_ge(sgp.h, stats_m)
                    tc = slice(t, t + 1)
                    if use_xn:
                        sc.activation(d0a1[:], d0x[t % 2][:], AF.Identity,
                                      bias=nm0[:, tc], scale=rsq0[:, tc],
                                      accum_out=d0sx[:, tc])
                        sc.activation(d0a2[:], d0a1[:], AF.Square,
                                      accum_out=d0sxx[:, tc])
                    else:
                        sc.activation(d0a1[:], d0x[t % 2][:], AF.Copy,
                                      accum_out=d0sx[:, tc])
                        sc.activation(d0a2[:], d0x[t % 2][:], AF.Square,
                                      accum_out=d0sxx[:, tc])
                    sc.mark(ss)

            @on("gpsimd")
            def _(gp):
                gp.wait_ge(ss.h, ss.n)
                gp.wait_ge(sv.h, ph0_m)
                gp.tensor_scalar(d0sx[:], d0sx[:], 1.0 / DIM, None, AO.mult)
                gp.tensor_scalar(d0sxx[:], d0sxx[:], 1.0 / DIM, None, AO.mult)
                gp.tensor_tensor(d0tmp[:], d0sx[:], d0sx[:], AO.mult)
                gp.tensor_tensor(d0sxx[:], d0sxx[:], d0tmp[:], AO.subtract)
                gp.mark(sgp)

            dst_r = rsq2 if use_xn else rsq0
            dst_n = nm2 if use_xn else nm0

            @on("gpsimd")
            def _(gp, dst_r=dst_r, dst_n=dst_n):
                gp.tensor_scalar(d0sxx[:], d0sxx[:], 1e-5, None, AO.add)
                gp.tensor_tensor(dst_r[:], d0sxx[:], mh16[:], AO.pow)
                gp.tensor_scalar(d0tmp[:], d0sx[:], -1.0, None, AO.mult)
                sgp.inc(gp.tensor_tensor(dst_n[:], d0tmp[:], dst_r[:],
                                         AO.mult))
            return sgp.n

        quant_weight(jobs1, W1Q, sw1c, mw1c)
        d0a_m = d0_pass(False)
        quant_weight(jobs2, W2QT, sw2c, mw2c)
        d0_pass(True, ready_m=d0a_m)

        @on("gpsimd")
        def _(gp):
            gp.tensor_tensor(rsq02[:], rsq0[:], rsq2[:], AO.mult)
            gp.tensor_tensor(d0tmp[:], nm0[:], rsq2[:], AO.mult)
            sgp.inc(gp.tensor_tensor(nm02[:], d0tmp[:], nm2[:], AO.add))
        if debug:
            @on("gpsimd")
            def _(g):
                g.wait_ge(sv.h, sv.n)
                sgd.inc(g.dma_start(out=dbg["dbg_W1Q"],
                                    in_=W1Q[:, 0:4096]), 16)
                sgd.inc(g.dma_start(out=dbg["dbg_sw"][:, 0:1], in_=sw1c[:], allow_slow_non_contiguous=True), 16)
                sgd.inc(g.dma_start(out=dbg["dbg_sw"][:, 1:2], in_=mw1c[:], allow_slow_non_contiguous=True), 16)
                sgd.inc(g.dma_start(out=dbg["dbg_sw"][:, 2:3], in_=sw2c[:], allow_slow_non_contiguous=True), 16)
                sgd.inc(g.dma_start(out=dbg["dbg_sw"][:, 3:4], in_=mw2c[:], allow_slow_non_contiguous=True), 16)
                sgd.inc(g.dma_start(out=dbg["dbg_wsum"], in_=wsum[:]), 16)
        barrier()
        wst.close()

        dst_ = contextlib.ExitStack()
        xbuf = sb("xbuf", [128, DIM], st=dst_)
        q1buf = sb("q1buf", [128, DIM], BF16, st=dst_)
        q1t = sb("q1t", [128, DIM], BF16, st=dst_)
        ybc = sb("ybc", [128, HID], BF16, st=dst_)
        gbuf = sb("gbuf", [128, H_FFN], st=dst_)
        s1b = sb("s1b", [128, DIM], st=dst_)
        s2b = sb("s2b", [128, DIM], st=dst_)
        l2g2 = [sb("l2g2_%d" % i, [128, DIM], st=dst_) for i in range(2)]
        l2b2 = [sb("l2b2_%d" % i, [128, DIM], st=dst_) for i in range(2)]
        b1c2 = [sb("b1c2_%d" % i, [128, DIM], st=dst_) for i in range(2)]
        gs8 = sb("gs8", [128, 8], st=dst_)
        gq8 = sb("gq8", [128, 8], st=dst_)
        amx = sb("amx", [128, 4], st=dst_)
        xnv = gbuf[:, 0:1024]
        h1v = gbuf[:, 1024:2048]
        scr1 = gbuf[:, 2048:3072]
        ffnb = gbuf[:, 3072:4096]

        M = {}
        sgp_base = sgp.n
        for t in range(T_TILES):
            @on("sync")
            def _(sy, t=t, xn_m=M.get("xn", 0), l2_m=M.get("h2tt3", 0)):
                if t == 0:
                    sy.wait_ge(sv.h, sv.n)
                    sy.wait_ge(ss.h, ss.n)
                    sy.wait_ge(sgp.h, sgp.n)
                else:
                    sy.wait_ge(sv.h, max(xn_m, l2_m))
                sxc.inc(sy.dma_start(out=xbuf[:],
                                     in_=d_x[t * 128:(t + 1) * 128, :]), 16)
                par[0].inc(sy.dma_start(
                    out=l2g2[0][:], in_=d_l1g.partition_broadcast(128)), 16)
                par[0].inc(sy.dma_start(
                    out=l2b2[0][:], in_=d_l1b.partition_broadcast(128)), 16)
            ld_n = sxc.n
            g_lds = [par[0].n]

            @on("vector")
            def _(v, t=t, ld_n=ld_n, q2v1=M.get("q2v2", 0), sgd_n=sgd.n,
                  gl=g_lds[0], q2T3=M.get("q2T3", 0)):
                v.wait_ge(sxc.h, ld_n)
                v.wait_ge(sgd.h, sgd_n)
                v.wait_ge(par[0].h, gl)
                v.wait_ge(st_.h, q2T3)
                tc = slice(t, t + 1)
                v.tensor_scalar(h1v, xbuf[:], rsq02[:, tc], nm02[:, tc],
                                AO.mult, AO.add)
                v.mark(sv)
                M["xn"] = sv.n
                v.tensor_tensor(scr1, h1v, l2g2[0][:], AO.mult)
                v.tensor_tensor(h1v, scr1, l2b2[0][:], AO.add)
                v.tensor_reduce(stat1[:], h1v, axis=AX, op=AO.max,
                                apply_absolute_value=True)
                v.tensor_scalar(stat1[:], stat1[:], 1e-5, None, AO.max)
                v.reciprocal(stat3[:], stat1[:])
                v.tensor_scalar(stat2[:], stat3[:], 127.0, None, AO.mult)
                v.tensor_scalar(stat4[:], stat1[:], 1.0 / 127.0, None, AO.mult)
                yv = ybc[:].bitcast(F32)
                v.tensor_scalar(yv, h1v, stat2[:], MAGIC, AO.mult, AO.add)
                v.tensor_scalar(q1buf[:], yv, MAGIC, None, AO.subtract)
                v.mark(sv)
            q1_n = sv.n
            M["g1"] = sv.n

            def prefetch_l2(qs, gate):
                slot = (qs + 1) % 2

                @on("sync")
                def _(sy, qs=qs, slot=slot, gate=gate):
                    sy.wait_ge(sv.h, gate)
                    par[slot].inc(sy.dma_start(
                        out=l2g2[slot][:],
                        in_=d_l2g[:, qs * 1024:(qs + 1) * 1024]
                        .partition_broadcast(128)), 16)
                    par[slot].inc(sy.dma_start(
                        out=l2b2[slot][:],
                        in_=d_l2b[:, qs * 1024:(qs + 1) * 1024]
                        .partition_broadcast(128)), 16)
                g_lds.append(par[slot].n)

            prefetch_l2(0, M.get("h2tt2", 0))
            prefetch_l2(1, M["g1"])

            @on("tensor")
            def _(t_, q1_n=q1_n):
                t_.wait_ge(sv.h, q1_n)
                for d in range(8):
                    t_.transpose(pbf[:, d * 128:(d + 1) * 128],
                                 q1buf[:, d * 128:(d + 1) * 128], idb[:])
                t_.mark(st_)
            tq1_n = st_.n

            @on("vector")
            def _(v, tq1_n=tq1_n):
                v.wait_ge(st_.h, tq1_n)
                v.tensor_copy(q1t[:], pbf[:])
                v.mark(sv)
            q1t_m = sv.n

            ev_marks = {}
            sq_marks = {}
            b1_lds = {}
            gelu_base = ss.n
            for k in range(2):
                gate = M.get("mm2ev", 0) if k == 0 else M.get("ev7", 0)
                ss_gate = ss.n if t == 0 else 0

                @on("sync")
                def _(sy, k=k, gate=gate, ss_gate=ss_gate):
                    sy.wait_ge(sv.h, gate)
                    if ss_gate:
                        sy.wait_ge(ss.h, ss_gate)
                    parb[k % 2].inc(sy.dma_start(
                        out=b1c2[k][:],
                        in_=d_b1[:, k * 1024:(k + 1) * 1024]
                        .partition_broadcast(128)), 16)
                b1_lds[k] = parb[k % 2].n
            for hs512 in range(8):
                bank = pb[hs512 % 2]
                half = slice((hs512 % 2) * 512, (hs512 % 2) * 512 + 512)
                pair = hs512 // 2
                if hs512 % 2 == 0 and hs512 >= 2 and hs512 // 2 + 1 <= 3:
                    P = hs512 // 2 + 1
                    gate = ev_marks[2 * P - 3]

                    @on("sync")
                    def _(sy, P=P, gate=gate):
                        sy.wait_ge(sv.h, gate)
                        parb[P % 2].inc(sy.dma_start(
                            out=b1c2[P % 2][:],
                            in_=d_b1[:, P * 1024:(P + 1) * 1024]
                            .partition_broadcast(128)), 16)
                    b1_lds[P] = parb[P % 2].n

                @on("tensor")
                def _(t_, hs512=hs512, bank=bank,
                      gate=ev_marks.get(hs512 - 2, q1t_m)):
                    t_.wait_ge(sv.h, gate)
                    for d in range(8):
                        t_.matmul(bank[:],
                                  q1t[:, d * 128:(d + 1) * 128],
                                  W1Q[:, d * H_FFN + hs512 * 512:
                                      d * H_FFN + (hs512 + 1) * 512],
                                  start=(d == 0), stop=(d == 7))
                    t_.mark(st_)
                mm_n = st_.n

                @on("vector")
                def _(v, hs512=hs512, bank=bank, half=half, mm_n=mm_n,
                      b1_n=b1_lds[pair], pair=pair,
                      sqg=sq_marks.get(hs512 - 2, 0)):
                    v.wait_ge(st_.h, mm_n)
                    v.wait_ge(parb[pair % 2].h, b1_n)
                    v.wait_ge(ss.h, sqg)
                    v.tensor_scalar(s2b[:, half], bank[:], stat4[:], mw1c[:],
                                    AO.mult, AO.mult)
                    v.tensor_tensor(
                        s1b[:, half], s2b[:, half],
                        b1c2[pair % 2][:, half], AO.add)
                    v.mark(sv)
                ev_marks[hs512] = sv.n
                if hs512 == 7:
                    M["ev7"] = sv.n

                sq_marks[hs512] = gelu_base + hs512 + 1

            assert ss.n == gelu_base, "unexpected ss incs inside mm1 ladder"
            zid = []
            sq_seq = []
            idout = [s1b[:], s2b[:], s1b[:], s2b[:]]

            @on("scalar")
            def _(sc, evs=[ev_marks[h] for h in range(8)]):
                for hs512 in range(8):
                    sc.wait_ge(sv.h, evs[hs512])
                    half = slice((hs512 % 2) * 512, (hs512 % 2) * 512 + 512)
                    ss.inc(sc.activation(
                        gbuf[:, hs512 * 512:(hs512 + 1) * 512],
                        s1b[:, half], AF.Gelu,
                        accum_out=gs8[:, hs512:hs512 + 1]))

                dumps = [ybc[:].bitcast(F32)[:, 0:512],
                         ybc[:].bitcast(F32)[:, 512:1024],
                         s2b[:, 0:512], s2b[:, 512:1024]]
                for hs512 in range(8):
                    if hs512 == 4:
                        sc.drain()
                    ins = sc.activation(
                        dumps[hs512 % 4], gbuf[:, hs512 * 512:(hs512 + 1) * 512],
                        AF.Square, accum_out=gq8[:, hs512:hs512 + 1])
                    if hs512 == 7:
                        sq_seq.append(ss.inc(ins))
                sc.drain()
                sc.wait_ge(sgp.h, sgp_base + t + 1)
                for qs in range(2):
                    ss.inc(sc._e.activation(
                        idout[qs], gbuf[:, qs * 1024:(qs + 1) * 1024],
                        AF.Identity, bias=stat5[:], scale=stat4[:]))
                    zid.append(ss.n)
            sq_all = sq_seq[-1]

            @on("vector")
            def _(v, sq_last=sq_all):
                v.wait_ge(ss.h, sq_last)
                v.tensor_reduce(stat1[:], gs8[:], axis=AX, op=AO.add)
                v.tensor_reduce(stat2[:], gq8[:], axis=AX, op=AO.add)
                v.tensor_scalar(stat1[:], stat1[:], 1.0 / H_FFN, None, AO.mult)
                v.tensor_scalar(stat2[:], stat2[:], 1.0 / H_FFN, None, AO.mult)
                v.tensor_tensor(stat3[:], stat1[:], stat1[:], AO.mult)
                v.tensor_tensor(stat2[:], stat2[:], stat3[:], AO.subtract)
                v.mark(sv)

            @on("gpsimd")
            def _(gp):
                gp.wait_ge(sv.h, sv.n)
                gp.tensor_scalar(stat3[:], stat2[:], 1e-5, None, AO.add)
                gp.tensor_tensor(stat4[:], stat3[:], mh16[:, 0:1], AO.pow)
                gp.tensor_tensor(stat5[:], stat1[:], stat4[:], AO.mult)
                sgp.inc(gp.tensor_scalar(stat5[:], stat5[:], -1.0, None,
                                         AO.mult))
            assert sgp.n == sgp_base + t + 1, "sgp reservation broken"


            for qs in range(4):
                if qs >= 2:
                    prefetch_l2(qs, M["h2tt%d" % (qs - 2)])
                slot = (qs + 1) % 2
                if qs == 2:
                    @on("scalar")
                    def _(sc):
                        for q2 in (2, 3):
                            sc.wait_ge(sv.h, M["h2tt%d" % (q2 - 2)])
                            ss.inc(sc._e.activation(
                                idout[q2],
                                gbuf[:, q2 * 1024:(q2 + 1) * 1024],
                                AF.Identity, bias=stat5[:], scale=stat4[:]))
                            zid.append(ss.n)

                @on("vector")
                def _(v, qs=qs, slot=slot, l2_n=g_lds[qs + 1], idm=zid[qs]):
                    v.wait_ge(par[slot].h, l2_n)
                    v.wait_ge(ss.h, idm)
                    v.tensor_tensor(xbuf[:], idout[qs], l2g2[slot][:],
                                    AO.mult)
                    v.tensor_tensor(gbuf[:, qs * 1024:(qs + 1) * 1024],
                                    xbuf[:], l2b2[slot][:], AO.add)
                    v.tensor_reduce(amx[:, qs:qs + 1],
                                    gbuf[:, qs * 1024:(qs + 1) * 1024],
                                    axis=AX, op=AO.max,
                                    apply_absolute_value=True)
                    v.mark(sv)
                M["h2tt%d" % qs] = sv.n

            @on("vector")
            def _(v):
                v.tensor_reduce(stat1[:], amx[:], axis=AX, op=AO.max)
                v.tensor_scalar(stat1[:], stat1[:], 1e-5, None, AO.max)
                v.reciprocal(stat3[:], stat1[:])
                v.tensor_scalar(stat2[:], stat3[:], 127.0, None, AO.mult)
                v.tensor_scalar(stat5[:], stat1[:], 1.0 / 127.0, None, AO.mult)
                v.mark(sv)

            for qs in range(4):
                @on("vector")
                def _(v, qs=qs):
                    yv = ybc[:].bitcast(F32)
                    v.tensor_scalar(yv, gbuf[:, qs * 1024:(qs + 1) * 1024],
                                    stat2[:], MAGIC, AO.mult, AO.add)
                    v.tensor_scalar(q1buf[:], yv, MAGIC, None, AO.subtract)
                    v.mark(sv)
                    if qs < 2:
                        M["q2v%d" % (qs + 1)] = sv.n
                q2q_m = sv.n

                @on("tensor")
                def _(t_, qs=qs, q2q_m=q2q_m):
                    t_.wait_ge(sv.h, q2q_m)
                    for k in range(8):
                        t_.transpose(pbf[:, k * 128:(k + 1) * 128],
                                     q1buf[:, k * 128:(k + 1) * 128], idb[:])
                    t_.mark(st_)
                tq_n = st_.n
                if qs == 3:
                    M["q2T3"] = st_.n

                @on("vector")
                def _(v, qs=qs, tq_n=tq_n):
                    v.wait_ge(st_.h, tq_n)
                    v.tensor_copy(q1t[:], pbf[:])
                    v.mark(sv)
                q2s_n = sv.n

                @on("tensor")
                def _(t_, qs=qs, q2s_n=q2s_n):
                    t_.wait_ge(sv.h, q2s_n)
                    for k in range(8):
                        for ds in range(2):
                            t_.matmul(pb[2 + ds][:],
                                      q1t[:, k * 128:(k + 1) * 128],
                                      W2QT[:, (qs * 8 + k) * DIM + ds * 512:
                                           (qs * 8 + k) * DIM + ds * 512 + 512],
                                      start=(qs == 0 and k == 0),
                                      stop=(qs == 3 and k == 7))
                    t_.mark(st_)
            mm2_n = st_.n

            prev_v = sv.n

            @on("sync")
            def _(sy, g5=ev_marks[5]):
                sy.wait_ge(sv.h, g5)
                parb[0].inc(sy.dma_start(out=b1c2[0][:],
                                         in_=d_b2.partition_broadcast(128)), 16)
            b2_n = parb[0].n

            sgd_before = sgd.n

            @on("vector")
            def _(v, mm2_n=mm2_n, sgd_before=sgd_before, b2_n=b2_n):
                v.wait_ge(st_.h, mm2_n)
                v.wait_ge(sgd.h, sgd_before)
                v.wait_ge(parb[0].h, b2_n)
                for ds in range(2):
                    v.tensor_scalar(s1b[:, ds * 512:(ds + 1) * 512],
                                    pb[2 + ds][:], stat5[:], mw2c[:],
                                    AO.mult, AO.mult)
                v.tensor_tensor(ffnb, s1b[:], b1c2[0][:], AO.add)
                v.mark(sv)
            ffn_n = sv.n
            M["mm2ev"] = sv.n

            @on("gpsimd")
            def _(g, t=t, ffn_n=ffn_n):
                g.wait_ge(sv.h, ffn_n)
                sgd.inc(g.dma_start(out=d_out[t * 128:(t + 1) * 128, :],
                                    in_=ffnb), 16)

        if debug:
            @on("gpsimd")
            def _(g):
                g.wait_ge(sv.h, sv.n)
                sgd.inc(g.dma_start(out=dbg["dbg_stats"][:, 0:16], in_=rsq0[:]), 16)
                sgd.inc(g.dma_start(out=dbg["dbg_stats"][:, 16:32], in_=nm0[:]), 16)
                sgd.inc(g.dma_start(out=dbg["dbg_stats"][:, 32:48], in_=rsq2[:]), 16)
                sgd.inc(g.dma_start(out=dbg["dbg_stats"][:, 48:64], in_=nm2[:]), 16)

        barrier()
        dst_.close()
        ffnw_st.close()

        ntst = contextlib.ExitStack()
        NTh = sb("NTh", [128, 8 * 2048], BF16, st=ntst)
        NTl = sb("NTl", [128, 8 * 2048], BF16, st=ntst)
        atst = contextlib.ExitStack()
        AT = sb("AT", [128, 4096], st=atst)
        linKT = sb("linKT", [128, 2048], st=atst)
        biasTs = sb("biasTs", [128, 512], st=atst)
        ast = contextlib.ExitStack()
        WqkS = sb("WqkS", [128, 1024], st=ast)
        wqh = sb("wqh", [128, 1024], BF16, st=ast)
        wql = sb("wql", [128, 1024], BF16, st=ast)
        ntf = sb("ntf", [128, 2048], st=ast)
        xb2 = [sb("xb2_%d" % i, [128, DIM], st=ast) for i in range(2)]
        xnb2 = [sb("xnb2_%d" % i, [128, DIM], st=ast) for i in range(2)]
        zb2 = [sb("zb2_%d" % i, [128, DIM], st=ast) for i in range(2)]
        qkb = sb("qkb", [128, 256], st=ast)
        qq = sb("qq", [128, 256], st=ast)
        qk_ = sb("qk_", [128, 256], st=ast)
        lkt = sb("lkt", [128, 256], st=ast)
        sscr = sb("sscr", [128, 256], st=ast)

        @on("sync")
        def _(sy):
            saux.inc(sy.dma_start(out=biasTs[:], in_=d_bias), 16)
            for d in range(8):
                saux.inc(sy.dma_start(out=WqkS[:, d * 128:(d + 1) * 128],
                                      in_=d_Wqk[d * 128:(d + 1) * 128, :]), 16)
        wqk_n = saux.n

        @on("vector")
        def _(v):
            v.wait_ge(saux.h, wqk_n)
            v.tensor_scalar(biasTs[:], biasTs[:], SQRT_QKD, None, AO.mult)
            v.tensor_copy(wqh[:], WqkS[:])
            v.tensor_tensor(wql[:], WqkS[:], wqh[:], AO.subtract)
            v.mark(sv)
        bias_m = sv.n

        M_A1 = {}
        a1_sbase = ss.n
        a1_jobs = []
        for g in range(N_GROUPS):
            @on("sync")
            def _(sy, g=g):
                if g > 0:
                    sy.wait_ge(sv.h, sv.n)
                for i in range(2):
                    sld.inc(sy.dma_start(
                        out=xb2[i][:],
                        in_=d_x[(2 * g + i) * 128:(2 * g + i + 1) * 128, :]), 16)
            ld_n = sld.n

            @on("vector")
            def _(v, g=g, ld_n=ld_n):
                v.wait_ge(sld.h, ld_n)
                v.wait_ge(st_.h, st_.n)
                for i in range(2):
                    t = 2 * g + i
                    tc = slice(t, t + 1)
                    v.tensor_scalar(xnb2[i][:], xb2[i][:], rsq0[:, tc],
                                    nm0[:, tc], AO.mult, AO.add)
                    v.tensor_scalar(zb2[i][:], xnb2[i][:], rsq2[:, tc],
                                    nm2[:, tc], AO.mult, AO.add)
                v.mark(sv)
            z_n = sv.n

            @on("tensor")
            def _(t_, g=g, z_n=z_n, ev=M_A1.get("atlast", 0),
                  sl=M_A1.get("silu", 0)):
                t_.wait_ge(ss.h, sl)
                t_.wait_ge(sv.h, max(z_n, ev))
                for i in range(2):
                    for d in range(8):
                        s = i * 8 + d
                        t_.transpose(pb[s // 4][:, (s % 4) * 128:
                                                (s % 4) * 128 + 128],
                                     zb2[i][:, d * 128:(d + 1) * 128], idf[:])
                t_.mark(st_)
            tp_n = st_.n

            @on("vector")
            def _(v, g=g, tp_n=tp_n):
                v.wait_ge(st_.h, tp_n)
                for i in range(2):
                    for d in range(8):
                        s = i * 8 + d
                        v.tensor_scalar(ntf[:, s * 128:(s + 1) * 128],
                                        pb[s // 4][:, (s % 4) * 128:
                                                   (s % 4) * 128 + 128],
                                        ngT[:, d:d + 1], nbT[:, d:d + 1],
                                        AO.mult, AO.add)
                src4 = ntf[:].rearrange("p (i d c) -> p i d c", i=2, d=8)
                hi4 = NTh[:].rearrange("p (d x) -> p d x", d=8)[
                    :, :, g * 256:g * 256 + 256].rearrange(
                    "p d (i c) -> p i d c", i=2)
                lo4 = NTl[:].rearrange("p (d x) -> p d x", d=8)[
                    :, :, g * 256:g * 256 + 256].rearrange(
                    "p d (i c) -> p i d c", i=2)
                v.tensor_copy(hi4, src4)
                v.tensor_tensor(lo4, src4, hi4, AO.subtract)
                v.mark(sv)
            nt_n = sv.n
            M_A1["ntev"] = sv.n

            @on("tensor")
            def _(t_, g=g, nt_n=nt_n):
                t_.wait_ge(sv.h, nt_n)
                if g == 0:
                    t_.wait_ge(saux.h, wqk_n)
                for d in range(8):
                    cs = slice(d * 2048 + g * 256, d * 2048 + g * 256 + 256)
                    ws = slice(d * 128, (d + 1) * 128)
                    t_.matmul(pb[2][:, :256], wqh[:, ws], NTh[:, cs],
                              start=(d == 0), stop=False)
                    t_.matmul(pb[2][:, :256], wqh[:, ws], NTl[:, cs],
                              start=False, stop=False)
                    t_.matmul(pb[2][:, :256], wql[:, ws], NTh[:, cs],
                              start=False, stop=(d == 7))
                t_.mark(st_)
            qk_n = st_.n

            a1_jobs.append(qk_n)
            silu_n = a1_sbase + len(a1_jobs)
            M_A1["silu"] = silu_n

            @on("vector")
            def _(v, g=g, silu_n=silu_n):
                v.wait_ge(ss.h, silu_n)
                v.tensor_scalar(qq[:], qkb[:], gamT[:, 0:1], betT[:, 0:1],
                                AO.mult, AO.add)
                v.tensor_scalar(linQF[:, g * 256:(g + 1) * 256], qkb[:],
                                gamT[:, 1:2], betT[:, 1:2], AO.mult, AO.add)
                v.tensor_scalar(qk_[:], qkb[:], gamT[:, 2:3], betT[:, 2:3],
                                AO.mult, AO.add)
                v.tensor_scalar(lkt[:], qkb[:], gamT[:, 3:4], betT[:, 3:4],
                                AO.mult, AO.add)
                v.mark(sv)
            heads_n = sv.n

            @on("tensor")
            def _(t_, heads_n=heads_n):
                t_.wait_ge(sv.h, heads_n)
                for i in range(2):
                    t_.transpose(pb[5][:, i * 128:(i + 1) * 128],
                                 lkt[:, i * 128:(i + 1) * 128], idf[:])
                t_.mark(st_)
            lkT_n = st_.n

            @on("vector")
            def _(v, g=g, lkT_n=lkT_n):
                v.wait_ge(st_.h, lkT_n)
                v.tensor_copy(linKT[:, g * 256:(g + 1) * 256], pb[5][:, :256])
                v.mark(sv)
            M_A1["lkT"] = sv.n

            for jh in range(2):
                @on("tensor")
                def _(t_, jh=jh, heads_n=heads_n):
                    t_.wait_ge(sv.h, heads_n)
                    t_.matmul(pb[3 + jh][:, :256],
                              qk_[:, jh * 128:(jh + 1) * 128],
                              qq[:], start=True, stop=True)
                    t_.mark(st_)
                sim_n = st_.n

                @on("vector")
                def _(v, g=g, jh=jh, sim_n=sim_n):
                    v.wait_ge(st_.h, sim_n)
                    if g == 0 and jh == 0:
                        v.wait_ge(sv.h, bias_m)
                    v.scalar_tensor_tensor(sscr[:], pb[3 + jh][:, :256],
                                           1.0 / 256.0,
                                           biasTs[:, jh * 256:(jh + 1) * 256],
                                           AO.mult, AO.add)
                    col = g * 512 + jh * 256
                    v.scalar_tensor_tensor(AT[:, col:col + 256], sscr[:], 0.0,
                                           sscr[:], AO.max, AO.mult)
                    v.mark(sv)
                M_A1["atlast"] = sv.n

        assert ss.n == a1_sbase, "unexpected ss incs inside A1 loop"

        @on("scalar")
        def _(sc, jobs=list(a1_jobs)):
            for qk_n in jobs:
                sc.wait_ge(st_.h, qk_n)
                ss.inc(sc._e.activation(qkb[:], pb[2][:, :256], AF.Silu,
                                        bias=bqk[:]))

        if debug:
            @on("gpsimd")
            def _(g_):
                g_.wait_ge(sv.h, sv.n)
                sgd.inc(g_.dma_start(out=dbg["dbg_NT"], in_=NT[:]), 16)
                sgd.inc(g_.dma_start(out=dbg["dbg_AT"], in_=AT[:]), 16)
                sgd.inc(g_.dma_start(out=dbg["dbg_linQ"], in_=linQF[:]), 16)

        barrier()
        ast.close()

        a2st = contextlib.ExitStack()
        whvh = [sb("whvh%d" % i, [128, 4096], BF16, st=a2st) for i in range(2)]
        whvl = [sb("whvl%d" % i, [128, 4096], BF16, st=a2st) for i in range(2)]
        wstg = [sb("wstg%d" % i, [128, 4096], st=a2st) for i in range(2)]
        vbuf = [sb("vbuf%d" % i, [128, 512], st=a2st) for i in range(2)]
        vtmp = [sb("vtmp%d" % i, [128, 512], st=a2st) for i in range(2)]
        bhv = sb("bhv", [128, HID], st=a2st)
        qte = [sb("qte%d" % i, [128, 1024], st=a2st) for i in range(2)]

        @on("sync")
        def _(sy):
            saux.inc(sy.dma_start(out=bhv[:],
                                  in_=d_bh_v.partition_broadcast(128)), 16)
            for d in range(8):
                pA.inc(sy.dma_start(
                    out=wstg[0][:, d * 512:(d + 1) * 512],
                    in_=d_Wh[d * 128:(d + 1) * 128, 0:512]), 16)
        bhv_n = saux.n
        whv_lds = [pA.n]
        whv_dec = []

        def decompose_whv(es):
            @on("vector")
            def _(v, es=es, ld=whv_lds[es], fr=st_.n):
                v.wait_ge(par[es % 2].h, ld)
                v.wait_ge(st_.h, fr)
                v.tensor_copy(whvh[es % 2][:], wstg[es % 2][:])
                v.tensor_tensor(whvl[es % 2][:], wstg[es % 2][:],
                                whvh[es % 2][:], AO.subtract)
                v.mark(sv)
            whv_dec.append(sv.n)
        decompose_whv(0)

        a2_sbase = ss.n
        a2_jobs = []
        for es in range(4):
            if es + 1 < 4:
                used_n = st_.n

                @on("sync")
                def _(sy, es=es, used_n=used_n, dec=whv_dec[es]):
                    if es >= 1:
                        sy.wait_ge(st_.h, used_n)
                    sy.wait_ge(sv.h, dec)
                    for d in range(8):
                        par[(es + 1) % 2].inc(sy.dma_start(
                            out=wstg[(es + 1) % 2][:, d * 512:(d + 1) * 512],
                            in_=d_Wh[d * 128:(d + 1) * 128,
                                     (es + 1) * 512:(es + 2) * 512]), 16)
                whv_lds.append(par[(es + 1) % 2].n)
                decompose_whv(es + 1)

            def emit_vmm(es, t):
                vb = vbuf[t % 2]

                @on("tensor")
                def _(t_, es=es, t=t, dec=whv_dec[es]):
                    t_.wait_ge(sv.h, max(dec, vfree[t % 2]))
                    if es == 0 and t == 0:
                        t_.wait_ge(saux.h, bhv_n)
                    bank = pb[t % 2]
                    for d in range(8):
                        acs = slice(d * 2048 + t * 128, d * 2048 + t * 128 + 128)
                        wcs = slice(d * 512, (d + 1) * 512)
                        t_.matmul(bank[:], NTh[:, acs], whvh[es % 2][:, wcs],
                                  start=(d == 0), stop=False)
                        t_.matmul(bank[:], NTh[:, acs], whvl[es % 2][:, wcs],
                                  start=False, stop=False)
                        t_.matmul(bank[:], NTl[:, acs], whvh[es % 2][:, wcs],
                                  start=False, stop=(d == 7))
                    t_.mark(st_)
                vmm_n = st_.n

                @on("vector")
                def _(v, es=es, t=t, vmm_n=vmm_n):
                    v.wait_ge(st_.h, vmm_n)
                    v.tensor_tensor(vtmp[t % 2][:], pb[t % 2][:],
                                    bhv[:, es * 512:(es + 1) * 512], AO.add)
                    v.mark(sv)
                vfree[t % 2] = sv.n
                vt_n = sv.n

                a2_jobs.append((vt_n, t))
                vready[t % 2] = a2_sbase + len(a2_jobs)

            def emit_qt(es, t):
                g, jh = t // 2, t % 2
                vb = vbuf[t % 2]

                @on("tensor")
                def _(t_, es=es, t=t, g=g, jh=jh, vb=vb, v_n=vready[t % 2],
                      qev=qt_evac[0]):
                    t_.wait_ge(ss.h, v_n)
                    if jh == 0:
                        t_.wait_ge(sv.h, qev)
                    t_.matmul(pb[2][:], linKT[:, t * 128:(t + 1) * 128], vb[:],
                              start=(t == 0),
                              stop=(t == T_TILES - 1))
                    for ec in range(4):
                        t_.matmul(pb[3 + ec][:, :256],
                                  vb[:, ec * 128:(ec + 1) * 128],
                                  AT[:, g * 512 + jh * 256:
                                     g * 512 + jh * 256 + 256],
                                  start=(jh == 0), stop=(jh == 1))
                    t_.mark(st_)
                qt_n = st_.n

                if jh == 1:
                    sgd_b = sgd.n

                    @on("vector")
                    def _(v, g=g, qt_n=qt_n, sgd_b=sgd_b):
                        v.wait_ge(st_.h, qt_n)
                        v.wait_ge(sgd.h, sgd_b)
                        for ec in range(4):
                            v.tensor_copy(
                                qte[g % 2][:, ec * 256:(ec + 1) * 256],
                                pb[3 + ec][:, :256])
                        v.mark(sv)
                    qev_n = sv.n
                    qt_evac[0] = sv.n

                    @on("gpsimd")
                    def _(gp, es=es, g=g, qev_n=qev_n):
                        gp.wait_ge(sv.h, qev_n)
                        sgd.inc(gp.dma_start(
                            out=d_quadT[g, es * 4:es * 4 + 4].rearrange(
                                "e p c -> p e c"),
                            in_=qte[g % 2][:].rearrange(
                                "p (e c) -> p e c", e=4)), 16)

            vfree = [sv.n, sv.n]
            vready = [0, 0]
            if es == 0:
                qt_evac = [0]
            emit_vmm(es, 0)
            for t in range(1, T_TILES):
                emit_vmm(es, t)
                emit_qt(es, t - 1)
            emit_qt(es, T_TILES - 1)

            lkv_mm = st_.n

            @on("vector")
            def _(v, es=es, lkv_mm=lkv_mm):
                v.wait_ge(st_.h, lkv_mm)
                v.tensor_scalar(lkv[:, es * 512:(es + 1) * 512], pb[2][:],
                                1.0 / 4096.0, None, AO.mult)
                v.mark(sv)

        assert ss.n == a2_sbase, "unexpected ss incs inside A2 loop"

        @on("scalar")
        def _(sc, jobs=list(a2_jobs)):
            for vt_n, t in jobs:
                sc.wait_ge(sv.h, vt_n)
                ss.inc(sc._e.activation(vbuf[t % 2][:], vtmp[t % 2][:],
                                        AF.Silu))

        barrier()
        a2st.close()
        atst.close()

        @on("gpsimd")
        def _(gp):
            sgd.inc(gp.dma_start(out=t_arin.ap(), in_=lkv[:]), 16)
            gp.wait_ge(sgd.h, sgd.n)
            scc.inc(gp.collective_compute(
                "AllReduce", AO.add,
                replica_groups=[[0, 1], [2, 3], [4, 5], [6, 7]],
                ins=[t_arin.ap().opt()], outs=[t_arout.ap().opt()]))
            gp.wait_ge(scc.h, scc.n)
            sgd.inc(gp.dma_start(out=lkv[:], in_=t_arout.ap()), 16)
        ar_done = sgd.n

        if debug:
            @on("gpsimd")
            def _(gp):
                gp.wait_ge(sgd.h, ar_done)
                sgd.inc(gp.dma_start(out=dbg["dbg_linkv"], in_=lkv[:]), 16)

        c1wst = contextlib.ExitStack()
        WoSh = sb("WoSh", [128, 16 * 1024], BF16, st=c1wst, side="right")
        WoSl = sb("WoSl", [128, 16 * 1024], BF16, st=c1wst, side="right")
        wostg = sb("wostg", [128, 1024], st=c1wst, side="right")
        bob = sb("bob", [128, DIM], st=c1wst, side="right")
        c0st = contextlib.ExitStack()
        whg = [sb("whg%d" % i, [128, 1024], st=c0st) for i in range(2)]
        whgh = [sb("whgh%d" % i, [128, 1024], BF16, st=c0st) for i in range(2)]
        whgl = [sb("whgl%d" % i, [128, 1024], BF16, st=c0st) for i in range(2)]
        bhgS = sb("bhgS", [128, 16], st=c0st)
        gte = [sb("gte%d" % i, [128, 512], st=c0st) for i in range(2)]

        @on("sync")
        def _(sy):
            parb[1].inc(sy.dma_start(out=bob[:],
                                     in_=d_bo.partition_broadcast(128)), 16)
        bob_n = parb[1].n
        wo_dec = [sv.n]
        wo_ld = []
        for ecc in range(16):
            @on("sync")
            def _(sy, ecc=ecc, dec=wo_dec[ecc]):
                sy.wait_ge(sv.h, dec)
                parb[0].inc(sy.dma_start(out=wostg[:],
                                         in_=d_Wo[ecc * 128:(ecc + 1) * 128,
                                                  :]), 16)
            wo_ld.append(parb[0].n)

            @on("vector")
            def _(v, ecc=ecc, ld=wo_ld[ecc]):
                v.wait_ge(parb[0].h, ld)
                cs = slice(ecc * 1024, (ecc + 1) * 1024)
                v.tensor_copy(WoSh[:, cs], wostg[:])
                v.tensor_tensor(WoSl[:, cs], wostg[:], WoSh[:, cs],
                                AO.subtract)
                v.mark(sv)
            wo_dec.append(sv.n)
        wos_n = parb[1].n
        wos_dec_n = wo_dec[-1]

        @on("sync")
        def _(sy):
            saux.inc(sy.dma_start(out=bhgS[:], in_=d_bhg_t), 16)
            for d in range(8):
                pA.inc(sy.dma_start(
                    out=whg[0][:, d * 128:(d + 1) * 128],
                    in_=d_Wh[d * 128:(d + 1) * 128, HID:HID + 128]), 16)
        bhg_n = saux.n
        whg_lds = [pA.n]
        whg_dec = []

        def decompose_whg(ec):
            @on("vector")
            def _(v, ec=ec, ld=whg_lds[ec], fr=st_.n):
                v.wait_ge(par[ec % 2].h, ld)
                v.wait_ge(st_.h, fr)
                v.tensor_copy(whgh[ec % 2][:], whg[ec % 2][:])
                v.tensor_tensor(whgl[ec % 2][:], whg[ec % 2][:],
                                whgh[ec % 2][:], AO.subtract)
                v.mark(sv)
            whg_dec.append(sv.n)
        decompose_whg(0)

        c0_sbase = ss.n
        c0_gbase = sgd.n
        c0_jobs = []
        for ec in range(16):
            if ec + 1 < 16:
                used_n = st_.n

                @on("sync")
                def _(sy, ec=ec, used_n=used_n, dec=whg_dec[ec]):
                    if ec >= 1:
                        sy.wait_ge(st_.h, used_n)
                    sy.wait_ge(sv.h, dec)
                    for d in range(8):
                        par[(ec + 1) % 2].inc(sy.dma_start(
                            out=whg[(ec + 1) % 2][:, d * 128:(d + 1) * 128],
                            in_=d_Wh[d * 128:(d + 1) * 128,
                                     HID + (ec + 1) * 128:
                                     HID + (ec + 2) * 128]), 16)
                whg_lds.append(par[(ec + 1) % 2].n)
                decompose_whg(ec + 1)

            for ts4 in range(4):
                k = ec * 4 + ts4

                @on("tensor")
                def _(t_, ec=ec, ts4=ts4, dec=whg_dec[ec], k=k):
                    t_.wait_ge(sv.h, dec)
                    if k >= 2:
                        t_.wait_ge(ss.h, c0_sbase + k - 1)
                    bank = pb[ts4 % 2]
                    for d in range(8):
                        acs = slice(d * 2048 + ts4 * 512,
                                    d * 2048 + (ts4 + 1) * 512)
                        ws = slice(d * 128, (d + 1) * 128)
                        t_.matmul(bank[:], whgh[ec % 2][:, ws], NTh[:, acs],
                                  start=(d == 0), stop=False)
                        t_.matmul(bank[:], whgh[ec % 2][:, ws], NTl[:, acs],
                                  start=False, stop=False)
                        t_.matmul(bank[:], whgl[ec % 2][:, ws], NTh[:, acs],
                                  start=False, stop=(d == 7))
                    t_.mark(st_)
                c0_jobs.append((st_.n, k, ec))
                gev_n = c0_sbase + k + 1

                @on("gpsimd")
                def _(gp, ec=ec, ts4=ts4, gev_n=gev_n):
                    gp.wait_ge(ss.h, gev_n)
                    for half in range(2):
                        sgd.inc(gp.dma_start(
                            out=d_gateT[ts4 * 2 + half, ec],
                            in_=gte[ts4 % 2][:, half * 256:(half + 1) * 256]),
                            16)

        @on("scalar")
        def _(sc, jobs=list(c0_jobs)):
            for gmm_n, k, ec in jobs:
                sc.wait_ge(st_.h, gmm_n)
                if k == 0:
                    sc.wait_ge(saux.h, bhg_n)
                if k >= 1:
                    sc.wait_ge(sgd.h, c0_gbase + 32 * k)
                ss.inc(sc._e.activation(gte[k % 2][:], pb[k % 2][:],
                                        AF.Silu, bias=bhgS[:, k // 4:k // 4 + 1]))

        barrier()
        c0st.close()
        ntst.close()

        c1st = contextlib.ExitStack()
        qtg = [sb("qtg%d" % i, [128, 4096], st=c1st) for i in range(2)]
        gtg = [sb("gtg%d" % i, [128, 4096], st=c1st) for i in range(2)]
        ogs = sb("ogs", [128, 1024], st=c1st)
        ogh = [sb("ogh%d" % i, [128, 4096], BF16, st=c1st) for i in range(2)]
        ogl = [sb("ogl%d" % i, [128, 4096], BF16, st=c1st) for i in range(2)]
        xbc = [sb("xbc%d" % i, [128, DIM], st=c1st) for i in range(2)]
        xnbc = sb("xnbc", [128, DIM], st=c1st)
        outb = sb("outb", [128, DIM], st=c1st)

        def load_group_tiles(sy, gg, slot):
            for ec in range(16):
                par[slot].inc(sy.dma_start(
                    out=qtg[slot][:, ec * 256:(ec + 1) * 256],
                    in_=d_quadT[gg, ec]), 16)
                par[slot].inc(sy.dma_start(
                    out=gtg[slot][:, ec * 256:(ec + 1) * 256],
                    in_=d_gateT[gg, ec]), 16)

        @on("sync")
        def _(sy):
            load_group_tiles(sy, 0, 0)
            sxc.inc(sy.dma_start(out=xbc[0][:], in_=d_x[0:128, :]), 16)
            sxc.inc(sy.dma_start(out=xbc[1][:], in_=d_x[128:256, :]), 16)
        c1_lds = [pA.n]
        cx_lds = [sxc.n]
        wo_marks = {}
        ogev = {}

        def emit_lin_chunk(g, c, first):
            @on("tensor")
            def _(t_, g=g, c=c, first=first):
                if first:
                    t_.wait_ge(par[g % 2].h, c1_lds[g])
                    if g == 0:
                        t_.wait_ge(parb[1].h, wos_n)
                gate = ogev.get((g, c - 1), ogev.get((g - 1, 3), 0))
                t_.wait_ge(sv.h, gate)
                for k in range(4):
                    ec = 4 * c + k
                    t_.matmul(pb[k // 2][:, (k % 2) * 256:(k % 2) * 256 + 256],
                              lkv[:, ec * 128:(ec + 1) * 128],
                              linQF[:, g * 256:(g + 1) * 256],
                              start=True, stop=True)
                t_.mark(st_)
            lt_n = st_.n

            @on("vector")
            def _(v, g=g, c=c, lt_n=lt_n):
                v.wait_ge(st_.h, lt_n)
                for k in range(4):
                    ec = 4 * c + k
                    sl = slice(ec * 256, (ec + 1) * 256)
                    ssl = slice(k * 256, (k + 1) * 256)
                    v.tensor_tensor(ogs[:, ssl],
                                    pb[k // 2][:, (k % 2) * 256:
                                               (k % 2) * 256 + 256],
                                    qtg[g % 2][:, sl], AO.add)
                    v.tensor_tensor(ogs[:, ssl], ogs[:, ssl],
                                    gtg[g % 2][:, sl], AO.mult)
                sl4 = slice(4 * c * 256, 4 * (c + 1) * 256)
                v.tensor_copy(ogh[g % 2][:, sl4], ogs[:])
                v.tensor_tensor(ogl[g % 2][:, sl4], ogs[:],
                                ogh[g % 2][:, sl4], AO.subtract)
                v.mark(sv)
            ogev[(g, c)] = sv.n

        emit_lin_chunk(0, 0, True)
        emit_lin_chunk(0, 1, False)
        emit_lin_chunk(0, 2, False)
        emit_lin_chunk(0, 3, False)

        c1_sbase = ss.n
        c1_jobs = []
        for g in range(N_GROUPS):
            og_n = ogev[(g, 3)]
            if g + 1 < N_GROUPS:
                qg_done = ogev.get((g - 1, 3), 0)

                @on("sync")
                def _(sy, g=g, qg_done=qg_done):
                    sy.wait_ge(sv.h, qg_done)
                    load_group_tiles(sy, g + 1, (g + 1) % 2)
                c1_lds.append(par[(g + 1) % 2].n)

            @on("tensor")
            def _(t_, g=g, og_n=og_n, evac=wo_marks.get(g - 1, 0)):
                t_.wait_ge(sv.h, og_n if g else max(og_n, wos_dec_n))
                t_.wait_ge(sgp.h, evac)
                for ih in range(2):
                    for ds in range(2):
                        bank = pb[2 + ih * 2 + ds]
                        for ec in range(16):
                            acs = slice(ec * 256 + ih * 128,
                                        ec * 256 + ih * 128 + 128)
                            wcs = slice(ec * 1024 + ds * 512,
                                        ec * 1024 + ds * 512 + 512)
                            t_.matmul(bank[:], ogh[g % 2][:, acs],
                                      WoSh[:, wcs],
                                      start=(ec == 0), stop=False)
                            t_.matmul(bank[:], ogh[g % 2][:, acs],
                                      WoSl[:, wcs], start=False, stop=False)
                            t_.matmul(bank[:], ogl[g % 2][:, acs],
                                      WoSh[:, wcs], start=False,
                                      stop=(ec == 15))
                t_.mark(st_)
            wo_n = st_.n

            if g + 1 < N_GROUPS:
                for c in range(4):
                    emit_lin_chunk(g + 1, c, c == 0)

            for ih in range(2):
                t = 2 * g + ih
                tc = slice(t, t + 1)
                c1_jobs.append((wo_n, cx_lds[g], sgd.n, tc, ih))
                xn_n = c1_sbase + len(c1_jobs)

                @on("vector")
                def _(v, ih=ih, xn_n=xn_n):
                    v.wait_ge(ss.h, xn_n)
                    for ds in range(2):
                        sl = slice(ds * 512, (ds + 1) * 512)
                        v.tensor_tensor(outb[:, sl], pb[2 + ih * 2 + ds][:],
                                        bob[:, sl], AO.add)
                        v.tensor_tensor(outb[:, sl], outb[:, sl],
                                        xnbc[:, sl], AO.add)
                        v.tensor_tensor(outb[:, sl], outb[:, sl],
                                        xbc[ih][:, sl], AO.add)
                    v.mark(sv)
                ob_n = sv.n

                @on("gpsimd")
                def _(gp, t=t, ob_n=ob_n):
                    gp.wait_ge(sv.h, ob_n)
                    sgp.inc(gp.nop())
                    sgd.inc(gp.dma_start(out=d_out[t * 128:(t + 1) * 128, :],
                                         in_=outb[:], accum_op=AO.add), 16)
                wo_marks[g] = sgp.n

            if g + 1 < N_GROUPS:
                @on("sync")
                def _(sy, g=g):
                    sy.wait_ge(sgd.h, sgd.n)
                    sxc.inc(sy.dma_start(out=xbc[0][:],
                                         in_=d_x[(2 * g + 2) * 128:
                                                 (2 * g + 3) * 128, :]), 16)
                    sxc.inc(sy.dma_start(out=xbc[1][:],
                                         in_=d_x[(2 * g + 3) * 128:
                                                 (2 * g + 4) * 128, :]), 16)
                cx_lds.append(sxc.n)

        assert ss.n == c1_sbase, "unexpected ss incs inside C1 loop"

        @on("scalar")
        def _(sc, jobs=list(c1_jobs)):
            for wo_n, cx_n, sgd_b, tc, ih in jobs:
                sc.wait_ge(st_.h, wo_n)
                sc.wait_ge(sxc.h, cx_n)
                sc.wait_ge(sgd.h, sgd_b)
                ss.inc(sc._e.activation(xnbc[:], xbc[ih][:], AF.Identity,
                                        bias=nm0[:, tc], scale=rsq0[:, tc]))

        @on("gpsimd")
        def _(gp):
            gp.wait_ge(sgd.h, sgd.n)
            gp.wait_ge(sld.h, sld.n)

        barrier()
        c1st.close()
        c1wst.close()

    return nc


def _host_stage(inputs):
    import math
    f32 = np.float32
    x = np.asarray(inputs["L0_attn_x"], f32)
    b_h = np.asarray(inputs["L0_attn_b_h"], f32)
    rel = np.asarray(inputs["L0_attn_rel_emb"], f32)

    GSZ, NB, MAXD = 256, 32, 128
    pos = np.arange(GSZ)
    n = pos[:, None] - pos[None, :]
    ret = (n < 0).astype(np.int32) * (NB // 2)
    na = np.abs(n)
    me = (NB // 2) // 2
    small = na < me
    val = (np.log(np.maximum(na, 1).astype(f32) / me) /
           f32(math.log(MAXD / me)) * f32(NB // 2 - me)).astype(np.int32)
    val = np.minimum(me + val, NB // 2 - 1)
    bucket = ret + np.where(small, na, val)
    biasT = rel[bucket, 0].T
    biasTg = np.ascontiguousarray(
        np.concatenate([biasT[0:128, :], biasT[128:256, :]], axis=1), f32)

    shared = {
        "Wh": np.ascontiguousarray(np.asarray(inputs["L0_attn_W_h"], f32)),
        "Wqk": np.ascontiguousarray(np.asarray(inputs["L0_attn_W_qk"], f32)),
        "Wo": np.ascontiguousarray(np.asarray(inputs["L0_attn_W_o"], f32)),
        "W1T": np.ascontiguousarray(np.asarray(inputs["L0_ffn_w1"], f32).T),
        "W2T": np.ascontiguousarray(np.asarray(inputs["L0_ffn_w2"], f32).T),
        "bh_v": np.ascontiguousarray(b_h[:2048].reshape(1, -1)),
        "bhg_t": np.ascontiguousarray(b_h[2048:].reshape(16, 128).T),
        "bo_row": np.asarray(inputs["L0_attn_b_o"], f32).reshape(1, -1),
        "b1_row": np.asarray(inputs["L0_ffn_b1"], f32).reshape(1, -1),
        "b2_row": np.asarray(inputs["L0_ffn_b2"], f32).reshape(1, -1),
        "ln1g_row": np.asarray(inputs["L0_ffn_ln1_g"], f32).reshape(1, -1),
        "ln1b_row": np.asarray(inputs["L0_ffn_ln1_b"], f32).reshape(1, -1),
        "ln2g_row": np.asarray(inputs["L0_ffn_ln2_g"], f32).reshape(1, -1),
        "ln2b_row": np.asarray(inputs["L0_ffn_ln2_b"], f32).reshape(1, -1),
        "bqk_col": np.ascontiguousarray(
            np.asarray(inputs["L0_attn_b_qk"], f32).reshape(128, 1)),
        "gammaT": np.ascontiguousarray(
            np.asarray(inputs["L0_attn_qk_gamma"], f32).T),
        "betaT": np.ascontiguousarray(
            np.asarray(inputs["L0_attn_qk_beta"], f32).T),
        "normgT": np.ascontiguousarray(
            np.asarray(inputs["L0_attn_norm_g"], f32).reshape(8, 128).T),
        "normbT": np.ascontiguousarray(
            np.asarray(inputs["L0_attn_norm_b"], f32).reshape(8, 128).T),
        "biasTg": biasTg,
        "identf": np.eye(128, dtype=f32),
        "identb": np.eye(128, dtype=ml_dtypes.bfloat16),
        "ones_col": np.ones((128, 1), f32),
        "ones_row": np.ones((1, 128), f32),
    }
    in_maps = []
    for c in range(8):
        b, h = c // 2, c % 2
        im = dict(shared)
        im["x"] = np.ascontiguousarray(x[b, h * 2048:(h + 1) * 2048, :])
        in_maps.append(im)
    return in_maps


def kernel(**inputs):
    in_maps = _host_stage(inputs)
    nc = build(debug=False)
    res = run_bass_kernel_spmd(nc, in_maps, core_ids=list(range(8)))
    out = np.empty((4, 4096, 1024), np.float32)
    for c in range(8):
        b, h = c // 2, c % 2
        out[b, h * 2048:(h + 1) * 2048, :] = res.results[c]["out"]
    return out


# version 32 (final)
# speedup vs baseline: 1.0210x; 1.0210x over previous
import contextlib
import numpy as np
import ml_dtypes
import concourse.bass as bass
import concourse.mybir as mybir
from concourse.bass_utils import run_bass_kernel_spmd

F32 = mybir.dt.float32
BF16 = mybir.dt.bfloat16
AO = mybir.AluOpType
AF = mybir.ActivationFunctionType
AX = mybir.AxisListType.X

MAGIC = 12582912.0
SQRT_QKD = float(128) ** 0.5
T_TILES = 16
N_GROUPS = 8
DIM = 1024
HID = 2048
QKD = 128
H_FFN = 4096



class _HazardGuard:
    _PASS = ("wait_ge", "nop", "drain", "dma_start")

    def __init__(self, hazard=True):
        self._e = None
        self._hazard = hazard
        self._last = None
        self._hist = []

    def bind(self, eng):
        self._e = eng
        return self

    @staticmethod
    def _names(xs):
        s = []
        for x in xs:
            t = getattr(x, "tensor", None)
            if t is None:
                continue
            lo, hi = 0, 1 << 30
            try:
                fsz = x.free_size()
                fnb = x.free_nbytes()
                if x.ndim == 2 and x.stride() == 1 and fsz > 0:
                    esz = fnb // fsz
                    lo = x.offset * esz
                    hi = lo + fnb
            except Exception:
                pass
            s.append((t.name, lo, hi))
        return s

    @staticmethod
    def _clash(a, b):
        for na, la, ha in a:
            for nb, lb, hb in b:
                if na == nb and la < hb and lb < ha:
                    return True
        return False

    def _guarded(self, outs, ins, call):
        if self._hazard:
            w = self._names(outs)
            r = self._names(ins)
            for pw, pr in self._hist:
                if (self._clash(r, pw) or self._clash(w, pw)
                        or self._clash(w, pr)):
                    self._e.drain()
                    self._hist = []
                    break
            res = call()
            self._hist = (self._hist + [(w, r)])[-2:]
        else:
            res = call()
        self._last = res
        return res

    def mark(self, sem, k=1):
        sem.inc(self._last, k)

    def matmul(self, out, lhsT, rhs, **kw):
        return self._guarded([out], [lhsT, rhs],
                             lambda: self._e.matmul(out, lhsT, rhs, **kw))

    def transpose(self, out, in_, ident):
        return self._guarded([out], [in_, ident],
                             lambda: self._e.transpose(out, in_, ident))

    def __getattr__(self, name):
        return getattr(self._e, name)

    def wait_ge(self, *a, **k):
        return self._e.wait_ge(*a, **k)

    def nop(self, *a, **k):
        return self._e.nop(*a, **k)

    def drain(self, *a, **k):
        self._hist = []
        return self._e.drain(*a, **k)

    def dma_start(self, *a, **k):
        return self._e.dma_start(*a, **k)

    def memset(self, ap, c):
        return self._guarded([ap], [], lambda: self._e.memset(ap, c))

    def tensor_copy(self, out, in_):
        return self._guarded([out], [in_], lambda: self._e.tensor_copy(out, in_))

    def reciprocal(self, out, in_):
        return self._guarded([out], [in_], lambda: self._e.reciprocal(out, in_))

    def tensor_tensor(self, out, a, b, op):
        return self._guarded([out], [a, b],
                             lambda: self._e.tensor_tensor(out, a, b, op))

    def tensor_scalar(self, out, in0, s1, s2, op0, op1=None, accum_out=None):
        outs = [out] + ([accum_out] if accum_out is not None else [])
        kw = {}
        if op1 is not None:
            kw["op1"] = op1
        if accum_out is not None:
            kw["accum_out"] = accum_out
        return self._guarded(outs, [in0, s1, s2],
                             lambda: self._e.tensor_scalar(out, in0, s1, s2,
                                                           op0, **kw))

    def scalar_tensor_tensor(self, out, in0, scalar, in1, op0, op1,
                             accum_out=None):
        outs = [out] + ([accum_out] if accum_out is not None else [])
        kw = {"accum_out": accum_out} if accum_out is not None else {}
        return self._guarded(outs, [in0, scalar, in1],
                             lambda: self._e.scalar_tensor_tensor(
                                 out, in0, scalar, in1, op0, op1, **kw))

    def tensor_reduce(self, out, in_, axis, op, apply_absolute_value=None):
        kw = {}
        if apply_absolute_value is not None:
            kw["apply_absolute_value"] = apply_absolute_value
        return self._guarded([out], [in_],
                             lambda: self._e.tensor_reduce(out, in_, axis=axis,
                                                           op=op, **kw))

    def activation(self, out, in_, func, bias=None, scale=None, accum_out=None):
        outs = [out] + ([accum_out] if accum_out is not None else [])
        kw = {}
        if bias is not None:
            kw["bias"] = bias
        if scale is not None:
            kw["scale"] = scale
        if accum_out is not None:
            kw["accum_out"] = accum_out
        return self._guarded(outs, [in_, bias, scale],
                             lambda: self._e.activation(out, in_, func, **kw))


class SemT:
    def __init__(self, ctx, nc, name):
        self.h = ctx.enter_context(nc.semaphore(name))
        self.n = 0

    def inc(self, instr, k=1):
        instr.then_inc(self.h, k)
        self.n += k
        return self.n


def build(debug=False):
    nc = bass.Bass("TRN2", target_bir_lowering=False,
                   detect_race_conditions=False)

    d_x = nc.dram_tensor("x", [2048, DIM], F32, kind="ExternalInput").ap()
    d_Wh = nc.dram_tensor("Wh", [DIM, 2 * HID], F32, kind="ExternalInput").ap()
    d_Wqk = nc.dram_tensor("Wqk", [DIM, QKD], F32, kind="ExternalInput").ap()
    d_Wo = nc.dram_tensor("Wo", [HID, DIM], F32, kind="ExternalInput").ap()
    d_W1T = nc.dram_tensor("W1T", [DIM, H_FFN], F32, kind="ExternalInput").ap()
    d_W2T = nc.dram_tensor("W2T", [H_FFN, DIM], F32, kind="ExternalInput").ap()
    d_bh_v = nc.dram_tensor("bh_v", [1, HID], F32, kind="ExternalInput").ap()
    d_bhg_t = nc.dram_tensor("bhg_t", [128, 16], F32, kind="ExternalInput").ap()
    d_bo = nc.dram_tensor("bo_row", [1, DIM], F32, kind="ExternalInput").ap()
    d_b1 = nc.dram_tensor("b1_row", [1, H_FFN], F32, kind="ExternalInput").ap()
    d_b2 = nc.dram_tensor("b2_row", [1, DIM], F32, kind="ExternalInput").ap()
    d_l1g = nc.dram_tensor("ln1g_row", [1, DIM], F32, kind="ExternalInput").ap()
    d_l1b = nc.dram_tensor("ln1b_row", [1, DIM], F32, kind="ExternalInput").ap()
    d_l2g = nc.dram_tensor("ln2g_row", [1, H_FFN], F32, kind="ExternalInput").ap()
    d_l2b = nc.dram_tensor("ln2b_row", [1, H_FFN], F32, kind="ExternalInput").ap()
    d_bqk = nc.dram_tensor("bqk_col", [128, 1], F32, kind="ExternalInput").ap()
    d_gam = nc.dram_tensor("gammaT", [128, 4], F32, kind="ExternalInput").ap()
    d_bet = nc.dram_tensor("betaT", [128, 4], F32, kind="ExternalInput").ap()
    d_ngT = nc.dram_tensor("normgT", [128, 8], F32, kind="ExternalInput").ap()
    d_nbT = nc.dram_tensor("normbT", [128, 8], F32, kind="ExternalInput").ap()
    d_bias = nc.dram_tensor("biasTg", [128, 512], F32, kind="ExternalInput").ap()
    d_idf = nc.dram_tensor("identf", [128, 128], F32, kind="ExternalInput").ap()
    d_idb = nc.dram_tensor("identb", [128, 128], BF16, kind="ExternalInput").ap()
    d_onec = nc.dram_tensor("ones_col", [128, 1], F32, kind="ExternalInput").ap()
    d_oner = nc.dram_tensor("ones_row", [1, 128], F32, kind="ExternalInput").ap()

    d_out = nc.dram_tensor("out", [2048, DIM], F32, kind="ExternalOutput").ap()

    d_quadT = nc.dram_tensor("quadT_scr", [N_GROUPS, 16, 128, 256], F32,
                             kind="Internal").ap()
    d_gateT = nc.dram_tensor("gateT_scr", [N_GROUPS, 16, 128, 256], F32,
                             kind="Internal").ap()
    t_arin = nc.dram_tensor("ar_in", [128, HID], F32, kind="Internal")
    t_arout = nc.dram_tensor("ar_out", [128, HID], F32, kind="Internal")

    dbg = {}
    if debug:
        for nm, shp in [("dbg_NT", [128, 8 * 2048]), ("dbg_linkv", [128, HID]),
                        ("dbg_AT", [128, 4096]), ("dbg_linQ", [128, 2048]),
                        ("dbg_stats", [128, 64]), ("dbg_W1Q", [128, 4096]),
                        ("dbg_sw", [128, 4]), ("dbg_wsum", [128, 8]),
                        ("dbg_ffn", [2048, DIM])]:
            dbg[nm] = nc.dram_tensor(nm, shp, F32, kind="ExternalOutput").ap()

    ctx = contextlib.ExitStack()
    with ctx:
        def sb(name, shape, dt=F32, st=None, side=None):
            if side:
                return (st or ctx).enter_context(
                    nc.sbuf_tensor(name, shape, dt, side=side))
            return (st or ctx).enter_context(nc.sbuf_tensor(name, shape, dt))

        rsq0 = sb("rsq0", [128, T_TILES]); nm0 = sb("nm0", [128, T_TILES])
        rsq2 = sb("rsq2", [128, T_TILES]); nm2 = sb("nm2", [128, T_TILES])
        rsq02 = sb("rsq02", [128, T_TILES]); nm02 = sb("nm02", [128, T_TILES])
        gamT = sb("gamT", [128, 4]); betT = sb("betT", [128, 4])
        ngT = sb("ngT", [128, 8]); nbT = sb("nbT", [128, 8])
        bqk = sb("bqk", [128, 1]); onec = sb("onec", [128, 1])
        oner = sb("oner", [1, 128])
        idf = sb("idf", [128, 128]); idb = sb("idb", [128, 128], BF16)
        c127 = sb("c127", [128, 1])
        sw1c = sb("sw1c", [128, 1]); sw2c = sb("sw2c", [128, 1])
        mw1c = sb("mw1c", [128, 1]); mw2c = sb("mw2c", [128, 1])
        stat1 = sb("stat1", [128, 1]); stat2 = sb("stat2", [128, 1])
        stat3 = sb("stat3", [128, 1]); stat4 = sb("stat4", [128, 1])
        stat5 = sb("stat5", [128, 1]); eps_c = sb("eps_c", [128, 1])
        mh16 = sb("mh16", [128, T_TILES])
        lkv = sb("lkv", [128, HID])
        linQF = sb("linQF", [128, 2048])

        pb = [ctx.enter_context(nc.psum_tensor("pb%d" % i, [128, 512], F32))
              for i in range(7)]
        pbf = ctx.enter_context(nc.psum_tensor("pbf", [128, 1024], BF16))

        sems = {}
        for s in ("sld", "sgd", "sv", "ss", "st_", "scc", "sbar",
                  "pA", "pB", "pC", "pD", "saux", "sxc", "sgp",
                  "pE", "pF"):
            sems[s] = SemT(ctx, nc, s)
        sld, sgd, sv, ss, st_, scc, sbar = (sems[k] for k in
                                            ("sld", "sgd", "sv", "ss", "st_",
                                             "scc", "sbar"))
        pA, pB, saux, sxc, sgp = (sems[k] for k in
                                  ("pA", "pB", "saux", "sxc", "sgp"))
        par = [pA, pB]
        parb = [sems["pC"], sems["pD"]]
        pard = [sems["pE"], sems["pF"]]

        block = ctx.enter_context(nc.Block())
        guards = {"vector": _HazardGuard(), "scalar": _HazardGuard(),
                  "tensor": _HazardGuard(hazard=False),
                  "gpsimd": _HazardGuard()}

        def on(eng):
            def deco(fn):
                if eng in guards:
                    getattr(block, eng)(lambda e: fn(guards[eng].bind(e)))
                else:
                    getattr(block, eng)(fn)
            return deco

        def barrier():
            vals = [(sld, sld.n), (sgd, sgd.n), (ss, ss.n), (st_, st_.n),
                    (pA, pA.n), (pB, pB.n), (saux, saux.n), (sxc, sxc.n),
                    (sems["pC"], sems["pC"].n), (sems["pD"], sems["pD"].n)]

            @on("vector")
            def _(v):
                for s, n in vals:
                    v.wait_ge(s.h, n)
                sv.inc(v._e.nop() if isinstance(v, _HazardGuard) else v.nop())

            @on("gpsimd")
            def _(g):
                g.wait_ge(sv.h, sv.n)
                sbar.inc(g.drain())
            bar_n = sbar.n
            for eng in ("sync", "scalar", "tensor", "vector"):
                @on(eng)
                def _(e):
                    e.wait_ge(sbar.h, bar_n)

        @on("sync")
        def _(sy):
            for dst, src in [(gamT, d_gam), (betT, d_bet), (ngT, d_ngT),
                             (nbT, d_nbT), (bqk, d_bqk), (onec, d_onec),
                             (oner, d_oner), (idf, d_idf), (idb, d_idb)]:
                saux.inc(sy.dma_start(out=dst[:], in_=src), 16)
        n_const = saux.n

        @on("vector")
        def _(v):
            v.wait_ge(saux.h, n_const)
            v.memset(c127[:], 127.0)
            v.memset(eps_c[:], 1e-5)
            v.memset(mh16[:], -0.5)
            v.mark(sv)
        ph0_m = sv.n

        ffnw_st = contextlib.ExitStack()
        W1Q = sb("W1Q", [128, 8 * H_FFN], BF16, st=ffnw_st)
        W2QT = sb("W2QT", [128, 32 * DIM], BF16, st=ffnw_st)
        wst = contextlib.ExitStack()
        wbuf = [sb("wbuf%d" % i, [128, HID], st=wst) for i in range(2)]
        wsum = sb("wsum", [128, 8], st=wst)
        ybc_w = sb("ybc_w", [128, 2 * HID], BF16, st=wst)
        wscr = sb("wscr", [128, HID], st=wst)
        d0x = [sb("d0x%d" % i, [128, DIM], st=wst) for i in range(2)]
        d0a1 = sb("d0a1", [128, DIM], st=wst)
        d0a2 = sb("d0a2", [128, DIM], st=wst)
        d0sx = sb("d0sx", [128, T_TILES], st=wst)
        d0sxx = sb("d0sxx", [128, T_TILES], st=wst)
        d0tmp = sb("d0tmp", [128, T_TILES], st=wst)

        def quant_weight(jobs, dst, swc, mwc):
            nj = len(jobs)
            loads = []
            for j in range(min(2, nj)):
                @on("sync")
                def _(sy, j=j):
                    sy.wait_ge(sv.h, sv.n)
                    loads.append(par[j % 2].inc(sy.dma_start(
                        out=wbuf[j % 2][:, :jobs[j][2]], in_=jobs[j][0]), 16))
            for j in range(nj):
                ld_n = loads[j]
                ck = jobs[j][2]

                @on("vector")
                def _(v, j=j, ld_n=ld_n, ck=ck):
                    v.wait_ge(par[j % 2].h, ld_n)
                    col = j % 8
                    if j < 8:
                        v.tensor_reduce(wsum[:, col:col + 1],
                                        wbuf[j % 2][:, :ck], axis=AX,
                                        op=AO.add, apply_absolute_value=True)
                    else:
                        v.tensor_reduce(stat1[:], wbuf[j % 2][:, :ck],
                                        axis=AX, op=AO.add,
                                        apply_absolute_value=True)
                        v.tensor_tensor(wsum[:, col:col + 1],
                                        wsum[:, col:col + 1], stat1[:], AO.add)
                    v.mark(sv)
                if j + 2 < nj:
                    v_n = sv.n

                    @on("sync")
                    def _(sy, j=j, v_n=v_n):
                        sy.wait_ge(sv.h, v_n)
                        loads.append(par[j % 2].inc(sy.dma_start(
                            out=wbuf[j % 2][:, :jobs[j + 2][2]],
                            in_=jobs[j + 2][0]), 16))

            @on("vector")
            def _(v):
                v.tensor_reduce(stat1[:], wsum[:, :min(8, nj)], axis=AX,
                                op=AO.add)
                v.mark(sv)
            v_n = sv.n

            @on("tensor")
            def _(t):
                t.wait_ge(sv.h, v_n)
                t.matmul(pb[6][0:1, 0:1], onec[:], stat1[:], start=True,
                         stop=True)
                t.mark(st_)
            t_n = st_.n

            @on("vector")
            def _(v, t_n=t_n):
                v.wait_ge(st_.h, t_n)
                v.tensor_scalar(stat2[0:1, :], pb[6][0:1, 0:1],
                                1.0 / 4194304.0, 1e-5, AO.mult, AO.max)
                v.reciprocal(stat3[0:1, :], stat2[0:1, :])
                v.mark(sv)
            v_n = sv.n

            @on("tensor")
            def _(t, v_n=v_n):
                t.wait_ge(sv.h, v_n)
                t.matmul(pb[6][:, 1:2], oner[:], stat3[0:1, 0:1], start=True,
                         stop=True)
                t.matmul(pb[6][:, 2:3], oner[:], stat2[0:1, 0:1], start=True,
                         stop=True)
                t.mark(st_)
            t_n = st_.n

            @on("vector")
            def _(v, t_n=t_n):
                v.wait_ge(st_.h, t_n)
                v.tensor_copy(swc[:], pb[6][:, 1:2])
                v.tensor_copy(mwc[:], pb[6][:, 2:3])
                v.mark(sv)

            loads2 = []
            for j in range(min(2, nj)):
                @on("sync")
                def _(sy, j=j):
                    sy.wait_ge(sv.h, sv.n)
                    loads2.append(par[j % 2].inc(sy.dma_start(
                        out=wbuf[j % 2][:, :jobs[j][2]], in_=jobs[j][0]), 16))
            for j in range(nj):
                ld_n = loads2[j]
                ck = jobs[j][2]
                dcol = jobs[j][1]

                @on("vector")
                def _(v, j=j, ld_n=ld_n, ck=ck, dcol=dcol):
                    v.wait_ge(par[j % 2].h, ld_n)
                    yv = ybc_w[:].bitcast(F32)[:, :ck]
                    v.tensor_scalar(yv, wbuf[j % 2][:, :ck], swc[:],
                                    MAGIC, AO.mult, AO.add)
                    v.tensor_scalar(wscr[:, :ck], yv, MAGIC, -1.0,
                                    AO.subtract, AO.max)
                    v.tensor_scalar(dst[:, dcol:dcol + ck],
                                    wscr[:, :ck], 1.0, None, AO.min)
                    v.mark(sv)
                if j + 2 < nj:
                    v_n = sv.n

                    @on("sync")
                    def _(sy, j=j, v_n=v_n):
                        sy.wait_ge(sv.h, v_n)
                        loads2.append(par[j % 2].inc(sy.dma_start(
                            out=wbuf[j % 2][:, :jobs[j + 2][2]],
                            in_=jobs[j + 2][0]), 16))

        jobs1 = [(d_W1T[r * 128:(r + 1) * 128, hh * HID:(hh + 1) * HID],
                  (2 * r + hh) * HID, HID) for r in range(8) for hh in range(2)]
        jobs2 = [(d_W2T[r * 128:(r + 1) * 128, :], r * DIM, DIM)
                 for r in range(32)]
        def d0_pass(use_xn, ready_m=0):
            pard_base = [pard[0].n, pard[1].n]
            d0_sbase = ss.n
            for t in range(T_TILES):
                @on("gpsimd")
                def _(gp, t=t):
                    if use_xn and t == 0:
                        gp.wait_ge(ss.h, ss.n)
                    if t >= 2:
                        gp.wait_ge(ss.h, d0_sbase + t - 1)
                    pard[t % 2].inc(gp.dma_start(
                        out=d0x[t % 2][:],
                        in_=d_x[t * 128:(t + 1) * 128, :]), 16)

            @on("scalar")
            def _(sc, use_xn=use_xn, stats_m=ready_m):
                for t in range(T_TILES):
                    sc.wait_ge(pard[t % 2].h,
                               pard_base[t % 2] + 16 * (t // 2 + 1))
                    if use_xn and t == 0:
                        sc.wait_ge(sgp.h, stats_m)
                    tc = slice(t, t + 1)
                    if use_xn:
                        sc.activation(d0a1[:], d0x[t % 2][:], AF.Identity,
                                      bias=nm0[:, tc], scale=rsq0[:, tc],
                                      accum_out=d0sx[:, tc])
                        sc.activation(d0a2[:], d0a1[:], AF.Square,
                                      accum_out=d0sxx[:, tc])
                    else:
                        sc.activation(d0a1[:], d0x[t % 2][:], AF.Copy,
                                      accum_out=d0sx[:, tc])
                        sc.activation(d0a2[:], d0x[t % 2][:], AF.Square,
                                      accum_out=d0sxx[:, tc])
                    sc.mark(ss)

            @on("gpsimd")
            def _(gp):
                gp.wait_ge(ss.h, ss.n)
                gp.wait_ge(sv.h, ph0_m)
                gp.tensor_scalar(d0sx[:], d0sx[:], 1.0 / DIM, None, AO.mult)
                gp.tensor_scalar(d0sxx[:], d0sxx[:], 1.0 / DIM, None, AO.mult)
                gp.tensor_tensor(d0tmp[:], d0sx[:], d0sx[:], AO.mult)
                gp.tensor_tensor(d0sxx[:], d0sxx[:], d0tmp[:], AO.subtract)
                gp.mark(sgp)

            dst_r = rsq2 if use_xn else rsq0
            dst_n = nm2 if use_xn else nm0

            @on("gpsimd")
            def _(gp, dst_r=dst_r, dst_n=dst_n):
                gp.tensor_scalar(d0sxx[:], d0sxx[:], 1e-5, None, AO.add)
                gp.tensor_tensor(dst_r[:], d0sxx[:], mh16[:], AO.pow)
                gp.tensor_scalar(d0tmp[:], d0sx[:], -1.0, None, AO.mult)
                sgp.inc(gp.tensor_tensor(dst_n[:], d0tmp[:], dst_r[:],
                                         AO.mult))
            return sgp.n

        quant_weight(jobs1, W1Q, sw1c, mw1c)
        d0a_m = d0_pass(False)
        quant_weight(jobs2, W2QT, sw2c, mw2c)
        d0_pass(True, ready_m=d0a_m)

        @on("gpsimd")
        def _(gp):
            gp.tensor_tensor(rsq02[:], rsq0[:], rsq2[:], AO.mult)
            gp.tensor_tensor(d0tmp[:], nm0[:], rsq2[:], AO.mult)
            sgp.inc(gp.tensor_tensor(nm02[:], d0tmp[:], nm2[:], AO.add))
        if debug:
            @on("gpsimd")
            def _(g):
                g.wait_ge(sv.h, sv.n)
                sgd.inc(g.dma_start(out=dbg["dbg_W1Q"],
                                    in_=W1Q[:, 0:4096]), 16)
                sgd.inc(g.dma_start(out=dbg["dbg_sw"][:, 0:1], in_=sw1c[:], allow_slow_non_contiguous=True), 16)
                sgd.inc(g.dma_start(out=dbg["dbg_sw"][:, 1:2], in_=mw1c[:], allow_slow_non_contiguous=True), 16)
                sgd.inc(g.dma_start(out=dbg["dbg_sw"][:, 2:3], in_=sw2c[:], allow_slow_non_contiguous=True), 16)
                sgd.inc(g.dma_start(out=dbg["dbg_sw"][:, 3:4], in_=mw2c[:], allow_slow_non_contiguous=True), 16)
                sgd.inc(g.dma_start(out=dbg["dbg_wsum"], in_=wsum[:]), 16)
        barrier()
        wst.close()

        dst_ = contextlib.ExitStack()
        xbuf = sb("xbuf", [128, DIM], st=dst_)
        q1buf = sb("q1buf", [128, DIM], BF16, st=dst_)
        q1t = sb("q1t", [128, DIM], BF16, st=dst_)
        ybc = sb("ybc", [128, HID], BF16, st=dst_)
        gbuf = sb("gbuf", [128, H_FFN], st=dst_)
        s1b = sb("s1b", [128, DIM], st=dst_)
        s2b = sb("s2b", [128, DIM], st=dst_)
        l2g2 = [sb("l2g2_%d" % i, [128, DIM], st=dst_) for i in range(2)]
        l2b2 = [sb("l2b2_%d" % i, [128, DIM], st=dst_) for i in range(2)]
        b1c2 = [sb("b1c2_%d" % i, [128, DIM], st=dst_) for i in range(2)]
        gs8 = sb("gs8", [128, 8], st=dst_)
        gq8 = sb("gq8", [128, 8], st=dst_)
        amx = sb("amx", [128, 4], st=dst_)
        xnv = gbuf[:, 0:1024]
        h1v = gbuf[:, 1024:2048]
        scr1 = gbuf[:, 2048:3072]
        ffnb = gbuf[:, 3072:4096]

        M = {}
        sgp_base = sgp.n
        for t in range(T_TILES):
            @on("sync")
            def _(sy, t=t, xn_m=M.get("xn", 0), l2_m=M.get("h2tt3", 0)):
                if t == 0:
                    sy.wait_ge(sv.h, sv.n)
                    sy.wait_ge(ss.h, ss.n)
                    sy.wait_ge(sgp.h, sgp.n)
                else:
                    sy.wait_ge(sv.h, max(xn_m, l2_m))
                sxc.inc(sy.dma_start(out=xbuf[:],
                                     in_=d_x[t * 128:(t + 1) * 128, :]), 16)
                par[0].inc(sy.dma_start(
                    out=l2g2[0][:], in_=d_l1g.partition_broadcast(128)), 16)
                par[0].inc(sy.dma_start(
                    out=l2b2[0][:], in_=d_l1b.partition_broadcast(128)), 16)
            ld_n = sxc.n
            g_lds = [par[0].n]

            @on("vector")
            def _(v, t=t, ld_n=ld_n, q2v1=M.get("q2v2", 0), sgd_n=sgd.n,
                  gl=g_lds[0], q2T3=M.get("q2T3", 0)):
                v.wait_ge(sxc.h, ld_n)
                v.wait_ge(par[0].h, gl)
                v.wait_ge(st_.h, q2T3)
                tc = slice(t, t + 1)
                v.tensor_scalar(h1v, xbuf[:], rsq02[:, tc], nm02[:, tc],
                                AO.mult, AO.add)
                v.mark(sv)
                M["xn"] = sv.n
                v.tensor_tensor(scr1, h1v, l2g2[0][:], AO.mult)
                v.tensor_tensor(h1v, scr1, l2b2[0][:], AO.add)
                v.tensor_reduce(stat1[:], h1v, axis=AX, op=AO.max,
                                apply_absolute_value=True)
                v.tensor_scalar(stat1[:], stat1[:], 1e-5, None, AO.max)
                v.reciprocal(stat3[:], stat1[:])
                v.tensor_scalar(stat2[:], stat3[:], 127.0, None, AO.mult)
                v.tensor_scalar(stat4[:], stat1[:], 1.0 / 127.0, None, AO.mult)
                yv = ybc[:].bitcast(F32)
                v.tensor_scalar(yv, h1v, stat2[:], MAGIC, AO.mult, AO.add)
                v.tensor_scalar(q1buf[:], yv, MAGIC, None, AO.subtract)
                v.mark(sv)
            q1_n = sv.n
            M["g1"] = sv.n

            def prefetch_l2(qs, gate):
                slot = (qs + 1) % 2

                @on("sync")
                def _(sy, qs=qs, slot=slot, gate=gate):
                    sy.wait_ge(sv.h, gate)
                    par[slot].inc(sy.dma_start(
                        out=l2g2[slot][:],
                        in_=d_l2g[:, qs * 1024:(qs + 1) * 1024]
                        .partition_broadcast(128)), 16)
                    par[slot].inc(sy.dma_start(
                        out=l2b2[slot][:],
                        in_=d_l2b[:, qs * 1024:(qs + 1) * 1024]
                        .partition_broadcast(128)), 16)
                g_lds.append(par[slot].n)

            prefetch_l2(0, M.get("h2tt2", 0))
            prefetch_l2(1, M["g1"])

            @on("tensor")
            def _(t_, q1_n=q1_n):
                t_.wait_ge(sv.h, q1_n)
                for d in range(8):
                    t_.transpose(pbf[:, d * 128:(d + 1) * 128],
                                 q1buf[:, d * 128:(d + 1) * 128], idb[:])
                t_.mark(st_)
            tq1_n = st_.n

            @on("vector")
            def _(v, tq1_n=tq1_n):
                v.wait_ge(st_.h, tq1_n)
                v.tensor_copy(q1t[:], pbf[:])
                v.mark(sv)
            q1t_m = sv.n

            ev_marks = {}
            sq_marks = {}
            b1_lds = {}
            gelu_base = ss.n
            for k in range(2):
                gate = M.get("mm2ev", 0) if k == 0 else M.get("ev7", 0)
                ss_gate = ss.n if t == 0 else 0

                @on("sync")
                def _(sy, k=k, gate=gate, ss_gate=ss_gate):
                    sy.wait_ge(sv.h, gate)
                    if ss_gate:
                        sy.wait_ge(ss.h, ss_gate)
                    parb[k % 2].inc(sy.dma_start(
                        out=b1c2[k][:],
                        in_=d_b1[:, k * 1024:(k + 1) * 1024]
                        .partition_broadcast(128)), 16)
                b1_lds[k] = parb[k % 2].n
            for hs512 in range(8):
                bank = pb[hs512 % 2]
                half = slice((hs512 % 2) * 512, (hs512 % 2) * 512 + 512)
                pair = hs512 // 2
                if hs512 % 2 == 0 and hs512 >= 2 and hs512 // 2 + 1 <= 3:
                    P = hs512 // 2 + 1
                    gate = ev_marks[2 * P - 3]

                    @on("sync")
                    def _(sy, P=P, gate=gate):
                        sy.wait_ge(sv.h, gate)
                        parb[P % 2].inc(sy.dma_start(
                            out=b1c2[P % 2][:],
                            in_=d_b1[:, P * 1024:(P + 1) * 1024]
                            .partition_broadcast(128)), 16)
                    b1_lds[P] = parb[P % 2].n

                @on("tensor")
                def _(t_, hs512=hs512, bank=bank,
                      gate=ev_marks.get(hs512 - 2, q1t_m)):
                    t_.wait_ge(sv.h, gate)
                    for d in range(8):
                        t_.matmul(bank[:],
                                  q1t[:, d * 128:(d + 1) * 128],
                                  W1Q[:, d * H_FFN + hs512 * 512:
                                      d * H_FFN + (hs512 + 1) * 512],
                                  start=(d == 0), stop=(d == 7))
                    t_.mark(st_)
                mm_n = st_.n

                @on("vector")
                def _(v, hs512=hs512, bank=bank, half=half, mm_n=mm_n,
                      b1_n=b1_lds[pair], pair=pair, sgd_n=M.get("st_gd", 0),
                      sqg=sq_marks.get(hs512 - 2, 0)):
                    v.wait_ge(st_.h, mm_n)
                    if hs512 == 6:
                        v.wait_ge(sgd.h, sgd_n)
                    v.wait_ge(parb[pair % 2].h, b1_n)
                    v.wait_ge(ss.h, sqg)
                    v.tensor_scalar(s2b[:, half], bank[:], stat4[:], mw1c[:],
                                    AO.mult, AO.mult)
                    v.tensor_tensor(
                        s1b[:, half], s2b[:, half],
                        b1c2[pair % 2][:, half], AO.add)
                    v.mark(sv)
                ev_marks[hs512] = sv.n
                if hs512 == 7:
                    M["ev7"] = sv.n

                sq_marks[hs512] = gelu_base + hs512 + 1

            assert ss.n == gelu_base, "unexpected ss incs inside mm1 ladder"
            zid = []
            sq_seq = []
            idout = [s1b[:], s2b[:], s1b[:], s2b[:]]

            @on("scalar")
            def _(sc, evs=[ev_marks[h] for h in range(8)]):
                for hs512 in range(8):
                    sc.wait_ge(sv.h, evs[hs512])
                    half = slice((hs512 % 2) * 512, (hs512 % 2) * 512 + 512)
                    ss.inc(sc.activation(
                        gbuf[:, hs512 * 512:(hs512 + 1) * 512],
                        s1b[:, half], AF.Gelu,
                        accum_out=gs8[:, hs512:hs512 + 1]))

                dumps = [ybc[:].bitcast(F32)[:, 0:512],
                         ybc[:].bitcast(F32)[:, 512:1024],
                         s2b[:, 0:512], s2b[:, 512:1024]]
                for hs512 in range(8):
                    if hs512 == 4:
                        sc.drain()
                    ins = sc.activation(
                        dumps[hs512 % 4], gbuf[:, hs512 * 512:(hs512 + 1) * 512],
                        AF.Square, accum_out=gq8[:, hs512:hs512 + 1])
                    if hs512 == 7:
                        sq_seq.append(ss.inc(ins))
                sc.drain()
                sc.wait_ge(sgp.h, sgp_base + t + 1)
                for qs in range(2):
                    ss.inc(sc._e.activation(
                        idout[qs], gbuf[:, qs * 1024:(qs + 1) * 1024],
                        AF.Identity, bias=stat5[:], scale=stat4[:]))
                    zid.append(ss.n)
            sq_all = sq_seq[-1]

            @on("vector")
            def _(v, sq_last=sq_all):
                v.wait_ge(ss.h, sq_last)
                v.tensor_reduce(stat1[:], gs8[:], axis=AX, op=AO.add)
                v.tensor_reduce(stat2[:], gq8[:], axis=AX, op=AO.add)
                v.tensor_scalar(stat1[:], stat1[:], 1.0 / H_FFN, None, AO.mult)
                v.tensor_scalar(stat2[:], stat2[:], 1.0 / H_FFN, None, AO.mult)
                v.tensor_tensor(stat3[:], stat1[:], stat1[:], AO.mult)
                v.tensor_tensor(stat2[:], stat2[:], stat3[:], AO.subtract)
                v.mark(sv)

            @on("gpsimd")
            def _(gp):
                gp.wait_ge(sv.h, sv.n)
                gp.tensor_scalar(stat3[:], stat2[:], 1e-5, None, AO.add)
                gp.tensor_tensor(stat4[:], stat3[:], mh16[:, 0:1], AO.pow)
                gp.tensor_tensor(stat5[:], stat1[:], stat4[:], AO.mult)
                sgp.inc(gp.tensor_scalar(stat5[:], stat5[:], -1.0, None,
                                         AO.mult))
            assert sgp.n == sgp_base + t + 1, "sgp reservation broken"


            for qs in range(4):
                if qs >= 2:
                    prefetch_l2(qs, M["h2tt%d" % (qs - 2)])
                slot = (qs + 1) % 2
                if qs == 2:
                    @on("scalar")
                    def _(sc):
                        for q2 in (2, 3):
                            sc.wait_ge(sv.h, M["h2tt%d" % (q2 - 2)])
                            ss.inc(sc._e.activation(
                                idout[q2],
                                gbuf[:, q2 * 1024:(q2 + 1) * 1024],
                                AF.Identity, bias=stat5[:], scale=stat4[:]))
                            zid.append(ss.n)

                @on("vector")
                def _(v, qs=qs, slot=slot, l2_n=g_lds[qs + 1], idm=zid[qs]):
                    v.wait_ge(par[slot].h, l2_n)
                    v.wait_ge(ss.h, idm)
                    v.tensor_tensor(xbuf[:], idout[qs], l2g2[slot][:],
                                    AO.mult)
                    v.tensor_tensor(gbuf[:, qs * 1024:(qs + 1) * 1024],
                                    xbuf[:], l2b2[slot][:], AO.add)
                    v.tensor_reduce(amx[:, qs:qs + 1],
                                    gbuf[:, qs * 1024:(qs + 1) * 1024],
                                    axis=AX, op=AO.max,
                                    apply_absolute_value=True)
                    v.mark(sv)
                M["h2tt%d" % qs] = sv.n

            @on("vector")
            def _(v):
                v.tensor_reduce(stat1[:], amx[:], axis=AX, op=AO.max)
                v.tensor_scalar(stat1[:], stat1[:], 1e-5, None, AO.max)
                v.reciprocal(stat3[:], stat1[:])
                v.tensor_scalar(stat2[:], stat3[:], 127.0, None, AO.mult)
                v.tensor_scalar(stat5[:], stat1[:], 1.0 / 127.0, None, AO.mult)
                v.mark(sv)

            for qs in range(4):
                @on("vector")
                def _(v, qs=qs):
                    yv = ybc[:].bitcast(F32)
                    v.tensor_scalar(yv, gbuf[:, qs * 1024:(qs + 1) * 1024],
                                    stat2[:], MAGIC, AO.mult, AO.add)
                    v.tensor_scalar(q1buf[:], yv, MAGIC, None, AO.subtract)
                    v.mark(sv)
                    if qs < 2:
                        M["q2v%d" % (qs + 1)] = sv.n
                q2q_m = sv.n

                @on("tensor")
                def _(t_, qs=qs, q2q_m=q2q_m):
                    t_.wait_ge(sv.h, q2q_m)
                    for k in range(8):
                        t_.transpose(pbf[:, k * 128:(k + 1) * 128],
                                     q1buf[:, k * 128:(k + 1) * 128], idb[:])
                    t_.mark(st_)
                tq_n = st_.n
                if qs == 3:
                    M["q2T3"] = st_.n

                @on("vector")
                def _(v, qs=qs, tq_n=tq_n):
                    v.wait_ge(st_.h, tq_n)
                    v.tensor_copy(q1t[:], pbf[:])
                    v.mark(sv)
                q2s_n = sv.n

                @on("tensor")
                def _(t_, qs=qs, q2s_n=q2s_n):
                    t_.wait_ge(sv.h, q2s_n)
                    for k in range(8):
                        for ds in range(2):
                            t_.matmul(pb[2 + ds][:],
                                      q1t[:, k * 128:(k + 1) * 128],
                                      W2QT[:, (qs * 8 + k) * DIM + ds * 512:
                                           (qs * 8 + k) * DIM + ds * 512 + 512],
                                      start=(qs == 0 and k == 0),
                                      stop=(qs == 3 and k == 7))
                    t_.mark(st_)
            mm2_n = st_.n

            prev_v = sv.n

            @on("sync")
            def _(sy, g5=ev_marks[5]):
                sy.wait_ge(sv.h, g5)
                parb[0].inc(sy.dma_start(out=b1c2[0][:],
                                         in_=d_b2.partition_broadcast(128)), 16)
            b2_n = parb[0].n

            sgd_before = sgd.n

            @on("vector")
            def _(v, mm2_n=mm2_n, sgd_before=sgd_before, b2_n=b2_n):
                v.wait_ge(st_.h, mm2_n)
                v.wait_ge(sgd.h, sgd_before)
                v.wait_ge(parb[0].h, b2_n)
                for ds in range(2):
                    v.tensor_scalar(s1b[:, ds * 512:(ds + 1) * 512],
                                    pb[2 + ds][:], stat5[:], mw2c[:],
                                    AO.mult, AO.mult)
                v.tensor_tensor(ffnb, s1b[:], b1c2[0][:], AO.add)
                v.mark(sv)
            ffn_n = sv.n
            M["mm2ev"] = sv.n

            @on("gpsimd")
            def _(g, t=t, ffn_n=ffn_n):
                g.wait_ge(sv.h, ffn_n)
                sgd.inc(g.dma_start(out=d_out[t * 128:(t + 1) * 128, :],
                                    in_=ffnb), 16)
            M["st_gd"] = sgd.n

        if debug:
            @on("gpsimd")
            def _(g):
                g.wait_ge(sv.h, sv.n)
                sgd.inc(g.dma_start(out=dbg["dbg_stats"][:, 0:16], in_=rsq0[:]), 16)
                sgd.inc(g.dma_start(out=dbg["dbg_stats"][:, 16:32], in_=nm0[:]), 16)
                sgd.inc(g.dma_start(out=dbg["dbg_stats"][:, 32:48], in_=rsq2[:]), 16)
                sgd.inc(g.dma_start(out=dbg["dbg_stats"][:, 48:64], in_=nm2[:]), 16)

        barrier()
        dst_.close()
        ffnw_st.close()

        ntst = contextlib.ExitStack()
        NTh = sb("NTh", [128, 8 * 2048], BF16, st=ntst)
        NTl = sb("NTl", [128, 8 * 2048], BF16, st=ntst)
        atst = contextlib.ExitStack()
        AT = sb("AT", [128, 4096], st=atst)
        linKT = sb("linKT", [128, 2048], st=atst)
        biasTs = sb("biasTs", [128, 512], st=atst)
        ast = contextlib.ExitStack()
        WqkS = sb("WqkS", [128, 1024], st=ast)
        wqh = sb("wqh", [128, 1024], BF16, st=ast)
        wql = sb("wql", [128, 1024], BF16, st=ast)
        ntf = sb("ntf", [128, 2048], st=ast)
        xb2 = [sb("xb2_%d" % i, [128, DIM], st=ast) for i in range(2)]
        xnb2 = [sb("xnb2_%d" % i, [128, DIM], st=ast) for i in range(2)]
        zb2 = [sb("zb2_%d" % i, [128, DIM], st=ast) for i in range(2)]
        qkb = sb("qkb", [128, 256], st=ast)
        qq = sb("qq", [128, 256], st=ast)
        qk_ = sb("qk_", [128, 256], st=ast)
        lkt = sb("lkt", [128, 256], st=ast)
        sscr = sb("sscr", [128, 256], st=ast)

        @on("sync")
        def _(sy):
            saux.inc(sy.dma_start(out=biasTs[:], in_=d_bias), 16)
            for d in range(8):
                saux.inc(sy.dma_start(out=WqkS[:, d * 128:(d + 1) * 128],
                                      in_=d_Wqk[d * 128:(d + 1) * 128, :]), 16)
        wqk_n = saux.n

        @on("vector")
        def _(v):
            v.wait_ge(saux.h, wqk_n)
            v.tensor_scalar(biasTs[:], biasTs[:], SQRT_QKD, None, AO.mult)
            v.tensor_copy(wqh[:], WqkS[:])
            v.tensor_tensor(wql[:], WqkS[:], wqh[:], AO.subtract)
            v.mark(sv)
        bias_m = sv.n

        M_A1 = {}
        a1_sbase = ss.n
        a1_jobs = []
        for g in range(N_GROUPS):
            @on("sync")
            def _(sy, g=g):
                if g > 0:
                    sy.wait_ge(sv.h, sv.n)
                for i in range(2):
                    sld.inc(sy.dma_start(
                        out=xb2[i][:],
                        in_=d_x[(2 * g + i) * 128:(2 * g + i + 1) * 128, :]), 16)
            ld_n = sld.n

            @on("vector")
            def _(v, g=g, ld_n=ld_n):
                v.wait_ge(sld.h, ld_n)
                v.wait_ge(st_.h, st_.n)
                for i in range(2):
                    t = 2 * g + i
                    tc = slice(t, t + 1)
                    v.tensor_scalar(xnb2[i][:], xb2[i][:], rsq0[:, tc],
                                    nm0[:, tc], AO.mult, AO.add)
                    v.tensor_scalar(zb2[i][:], xnb2[i][:], rsq2[:, tc],
                                    nm2[:, tc], AO.mult, AO.add)
                v.mark(sv)
            z_n = sv.n

            @on("tensor")
            def _(t_, g=g, z_n=z_n, ev=M_A1.get("atlast", 0),
                  sl=M_A1.get("silu", 0)):
                t_.wait_ge(ss.h, sl)
                t_.wait_ge(sv.h, max(z_n, ev))
                for i in range(2):
                    for d in range(8):
                        s = i * 8 + d
                        t_.transpose(pb[s // 4][:, (s % 4) * 128:
                                                (s % 4) * 128 + 128],
                                     zb2[i][:, d * 128:(d + 1) * 128], idf[:])
                t_.mark(st_)
            tp_n = st_.n

            @on("vector")
            def _(v, g=g, tp_n=tp_n):
                v.wait_ge(st_.h, tp_n)
                for i in range(2):
                    for d in range(8):
                        s = i * 8 + d
                        v.tensor_scalar(ntf[:, s * 128:(s + 1) * 128],
                                        pb[s // 4][:, (s % 4) * 128:
                                                   (s % 4) * 128 + 128],
                                        ngT[:, d:d + 1], nbT[:, d:d + 1],
                                        AO.mult, AO.add)
                src4 = ntf[:].rearrange("p (i d c) -> p i d c", i=2, d=8)
                hi4 = NTh[:].rearrange("p (d x) -> p d x", d=8)[
                    :, :, g * 256:g * 256 + 256].rearrange(
                    "p d (i c) -> p i d c", i=2)
                lo4 = NTl[:].rearrange("p (d x) -> p d x", d=8)[
                    :, :, g * 256:g * 256 + 256].rearrange(
                    "p d (i c) -> p i d c", i=2)
                v.tensor_copy(hi4, src4)
                v.tensor_tensor(lo4, src4, hi4, AO.subtract)
                v.mark(sv)
            nt_n = sv.n
            M_A1["ntev"] = sv.n

            @on("tensor")
            def _(t_, g=g, nt_n=nt_n):
                t_.wait_ge(sv.h, nt_n)
                if g == 0:
                    t_.wait_ge(saux.h, wqk_n)
                for d in range(8):
                    cs = slice(d * 2048 + g * 256, d * 2048 + g * 256 + 256)
                    ws = slice(d * 128, (d + 1) * 128)
                    t_.matmul(pb[2][:, :256], wqh[:, ws], NTh[:, cs],
                              start=(d == 0), stop=False)
                    t_.matmul(pb[2][:, :256], wqh[:, ws], NTl[:, cs],
                              start=False, stop=False)
                    t_.matmul(pb[2][:, :256], wql[:, ws], NTh[:, cs],
                              start=False, stop=(d == 7))
                t_.mark(st_)
            qk_n = st_.n

            a1_jobs.append(qk_n)
            silu_n = a1_sbase + len(a1_jobs)
            M_A1["silu"] = silu_n

            @on("vector")
            def _(v, g=g, silu_n=silu_n):
                v.wait_ge(ss.h, silu_n)
                v.tensor_scalar(qq[:], qkb[:], gamT[:, 0:1], betT[:, 0:1],
                                AO.mult, AO.add)
                v.tensor_scalar(linQF[:, g * 256:(g + 1) * 256], qkb[:],
                                gamT[:, 1:2], betT[:, 1:2], AO.mult, AO.add)
                v.tensor_scalar(qk_[:], qkb[:], gamT[:, 2:3], betT[:, 2:3],
                                AO.mult, AO.add)
                v.tensor_scalar(lkt[:], qkb[:], gamT[:, 3:4], betT[:, 3:4],
                                AO.mult, AO.add)
                v.mark(sv)
            heads_n = sv.n

            @on("tensor")
            def _(t_, heads_n=heads_n):
                t_.wait_ge(sv.h, heads_n)
                for i in range(2):
                    t_.transpose(pb[5][:, i * 128:(i + 1) * 128],
                                 lkt[:, i * 128:(i + 1) * 128], idf[:])
                t_.mark(st_)
            lkT_n = st_.n

            @on("vector")
            def _(v, g=g, lkT_n=lkT_n):
                v.wait_ge(st_.h, lkT_n)
                v.tensor_copy(linKT[:, g * 256:(g + 1) * 256], pb[5][:, :256])
                v.mark(sv)
            M_A1["lkT"] = sv.n

            for jh in range(2):
                @on("tensor")
                def _(t_, jh=jh, heads_n=heads_n):
                    t_.wait_ge(sv.h, heads_n)
                    t_.matmul(pb[3 + jh][:, :256],
                              qk_[:, jh * 128:(jh + 1) * 128],
                              qq[:], start=True, stop=True)
                    t_.mark(st_)
                sim_n = st_.n

                @on("vector")
                def _(v, g=g, jh=jh, sim_n=sim_n):
                    v.wait_ge(st_.h, sim_n)
                    if g == 0 and jh == 0:
                        v.wait_ge(sv.h, bias_m)
                    v.scalar_tensor_tensor(sscr[:], pb[3 + jh][:, :256],
                                           1.0 / 256.0,
                                           biasTs[:, jh * 256:(jh + 1) * 256],
                                           AO.mult, AO.add)
                    col = g * 512 + jh * 256
                    v.scalar_tensor_tensor(AT[:, col:col + 256], sscr[:], 0.0,
                                           sscr[:], AO.max, AO.mult)
                    v.mark(sv)
                M_A1["atlast"] = sv.n

        assert ss.n == a1_sbase, "unexpected ss incs inside A1 loop"

        @on("scalar")
        def _(sc, jobs=list(a1_jobs)):
            for qk_n in jobs:
                sc.wait_ge(st_.h, qk_n)
                ss.inc(sc._e.activation(qkb[:], pb[2][:, :256], AF.Silu,
                                        bias=bqk[:]))

        if debug:
            @on("gpsimd")
            def _(g_):
                g_.wait_ge(sv.h, sv.n)
                sgd.inc(g_.dma_start(out=dbg["dbg_NT"], in_=NT[:]), 16)
                sgd.inc(g_.dma_start(out=dbg["dbg_AT"], in_=AT[:]), 16)
                sgd.inc(g_.dma_start(out=dbg["dbg_linQ"], in_=linQF[:]), 16)

        barrier()
        ast.close()

        a2st = contextlib.ExitStack()
        whvh = [sb("whvh%d" % i, [128, 4096], BF16, st=a2st) for i in range(2)]
        whvl = [sb("whvl%d" % i, [128, 4096], BF16, st=a2st) for i in range(2)]
        wstg = [sb("wstg%d" % i, [128, 4096], st=a2st) for i in range(2)]
        vbuf = [sb("vbuf%d" % i, [128, 512], st=a2st) for i in range(2)]
        vtmp = [sb("vtmp%d" % i, [128, 512], st=a2st) for i in range(2)]
        bhv = sb("bhv", [128, HID], st=a2st)
        qte = [sb("qte%d" % i, [128, 1024], st=a2st) for i in range(2)]

        @on("sync")
        def _(sy):
            saux.inc(sy.dma_start(out=bhv[:],
                                  in_=d_bh_v.partition_broadcast(128)), 16)
            for d in range(8):
                pA.inc(sy.dma_start(
                    out=wstg[0][:, d * 512:(d + 1) * 512],
                    in_=d_Wh[d * 128:(d + 1) * 128, 0:512]), 16)
        bhv_n = saux.n
        whv_lds = [pA.n]
        whv_dec = []

        def decompose_whv(es):
            @on("vector")
            def _(v, es=es, ld=whv_lds[es], fr=st_.n):
                v.wait_ge(par[es % 2].h, ld)
                v.wait_ge(st_.h, fr)
                v.tensor_copy(whvh[es % 2][:], wstg[es % 2][:])
                v.tensor_tensor(whvl[es % 2][:], wstg[es % 2][:],
                                whvh[es % 2][:], AO.subtract)
                v.mark(sv)
            whv_dec.append(sv.n)
        decompose_whv(0)

        a2_sbase = ss.n
        a2_jobs = []
        for es in range(4):
            if es + 1 < 4:
                used_n = st_.n

                @on("sync")
                def _(sy, es=es, used_n=used_n, dec=whv_dec[es]):
                    if es >= 1:
                        sy.wait_ge(st_.h, used_n)
                    sy.wait_ge(sv.h, dec)
                    for d in range(8):
                        par[(es + 1) % 2].inc(sy.dma_start(
                            out=wstg[(es + 1) % 2][:, d * 512:(d + 1) * 512],
                            in_=d_Wh[d * 128:(d + 1) * 128,
                                     (es + 1) * 512:(es + 2) * 512]), 16)
                whv_lds.append(par[(es + 1) % 2].n)
                decompose_whv(es + 1)

            def emit_vmm(es, t):
                vb = vbuf[t % 2]

                @on("tensor")
                def _(t_, es=es, t=t, dec=whv_dec[es]):
                    t_.wait_ge(sv.h, max(dec, vfree[t % 2]))
                    if es == 0 and t == 0:
                        t_.wait_ge(saux.h, bhv_n)
                    bank = pb[t % 2]
                    for d in range(8):
                        acs = slice(d * 2048 + t * 128, d * 2048 + t * 128 + 128)
                        wcs = slice(d * 512, (d + 1) * 512)
                        t_.matmul(bank[:], NTh[:, acs], whvh[es % 2][:, wcs],
                                  start=(d == 0), stop=False)
                        t_.matmul(bank[:], NTh[:, acs], whvl[es % 2][:, wcs],
                                  start=False, stop=False)
                        t_.matmul(bank[:], NTl[:, acs], whvh[es % 2][:, wcs],
                                  start=False, stop=(d == 7))
                    t_.mark(st_)
                vmm_n = st_.n

                @on("vector")
                def _(v, es=es, t=t, vmm_n=vmm_n):
                    v.wait_ge(st_.h, vmm_n)
                    v.tensor_tensor(vtmp[t % 2][:], pb[t % 2][:],
                                    bhv[:, es * 512:(es + 1) * 512], AO.add)
                    v.mark(sv)
                vfree[t % 2] = sv.n
                vt_n = sv.n

                a2_jobs.append((vt_n, t))
                vready[t % 2] = a2_sbase + len(a2_jobs)

            def emit_qt(es, t):
                g, jh = t // 2, t % 2
                vb = vbuf[t % 2]

                @on("tensor")
                def _(t_, es=es, t=t, g=g, jh=jh, vb=vb, v_n=vready[t % 2],
                      qev=qt_evac[0]):
                    t_.wait_ge(ss.h, v_n)
                    if jh == 0:
                        t_.wait_ge(sv.h, qev)
                    t_.matmul(pb[2][:], linKT[:, t * 128:(t + 1) * 128], vb[:],
                              start=(t == 0),
                              stop=(t == T_TILES - 1))
                    for ec in range(4):
                        t_.matmul(pb[3 + ec][:, :256],
                                  vb[:, ec * 128:(ec + 1) * 128],
                                  AT[:, g * 512 + jh * 256:
                                     g * 512 + jh * 256 + 256],
                                  start=(jh == 0), stop=(jh == 1))
                    t_.mark(st_)
                qt_n = st_.n

                if jh == 1:
                    sgd_b = sgd.n

                    @on("vector")
                    def _(v, g=g, qt_n=qt_n, sgd_b=sgd_b):
                        v.wait_ge(st_.h, qt_n)
                        v.wait_ge(sgd.h, sgd_b)
                        for ec in range(4):
                            v.tensor_copy(
                                qte[g % 2][:, ec * 256:(ec + 1) * 256],
                                pb[3 + ec][:, :256])
                        v.mark(sv)
                    qev_n = sv.n
                    qt_evac[0] = sv.n

                    @on("gpsimd")
                    def _(gp, es=es, g=g, qev_n=qev_n):
                        gp.wait_ge(sv.h, qev_n)
                        for ec in range(4):
                            sgd.inc(gp.dma_start(
                                out=d_quadT[g, es * 4 + ec],
                                in_=qte[g % 2][:, ec * 256:(ec + 1) * 256]), 16)

            vfree = [sv.n, sv.n]
            vready = [0, 0]
            if es == 0:
                qt_evac = [0]
            emit_vmm(es, 0)
            for t in range(1, T_TILES):
                emit_vmm(es, t)
                emit_qt(es, t - 1)
            emit_qt(es, T_TILES - 1)

            lkv_mm = st_.n

            @on("vector")
            def _(v, es=es, lkv_mm=lkv_mm):
                v.wait_ge(st_.h, lkv_mm)
                v.tensor_scalar(lkv[:, es * 512:(es + 1) * 512], pb[2][:],
                                1.0 / 4096.0, None, AO.mult)
                v.mark(sv)

        assert ss.n == a2_sbase, "unexpected ss incs inside A2 loop"

        @on("scalar")
        def _(sc, jobs=list(a2_jobs)):
            for vt_n, t in jobs:
                sc.wait_ge(sv.h, vt_n)
                ss.inc(sc._e.activation(vbuf[t % 2][:], vtmp[t % 2][:],
                                        AF.Silu))

        barrier()
        a2st.close()
        atst.close()

        @on("gpsimd")
        def _(gp):
            sgd.inc(gp.dma_start(out=t_arin.ap(), in_=lkv[:]), 16)
            gp.wait_ge(sgd.h, sgd.n)
            scc.inc(gp.collective_compute(
                "AllReduce", AO.add,
                replica_groups=[[0, 1], [2, 3], [4, 5], [6, 7]],
                ins=[t_arin.ap().opt()], outs=[t_arout.ap().opt()]))
            gp.wait_ge(scc.h, scc.n)
            sgd.inc(gp.dma_start(out=lkv[:], in_=t_arout.ap()), 16)
        ar_done = sgd.n

        if debug:
            @on("gpsimd")
            def _(gp):
                gp.wait_ge(sgd.h, ar_done)
                sgd.inc(gp.dma_start(out=dbg["dbg_linkv"], in_=lkv[:]), 16)

        c1wst = contextlib.ExitStack()
        WoSh = sb("WoSh", [128, 16 * 1024], BF16, st=c1wst, side="right")
        WoSl = sb("WoSl", [128, 16 * 1024], BF16, st=c1wst, side="right")
        wostg = sb("wostg", [128, 1024], st=c1wst, side="right")
        bob = sb("bob", [128, DIM], st=c1wst, side="right")
        c0st = contextlib.ExitStack()
        whg = [sb("whg%d" % i, [128, 1024], st=c0st) for i in range(2)]
        whgh = [sb("whgh%d" % i, [128, 1024], BF16, st=c0st) for i in range(2)]
        whgl = [sb("whgl%d" % i, [128, 1024], BF16, st=c0st) for i in range(2)]
        bhgS = sb("bhgS", [128, 16], st=c0st)
        gte = [sb("gte%d" % i, [128, 512], st=c0st) for i in range(2)]

        @on("sync")
        def _(sy):
            parb[1].inc(sy.dma_start(out=bob[:],
                                     in_=d_bo.partition_broadcast(128)), 16)
        bob_n = parb[1].n
        wo_dec = [sv.n]
        wo_ld = []
        for ecc in range(16):
            @on("sync")
            def _(sy, ecc=ecc, dec=wo_dec[ecc]):
                sy.wait_ge(sv.h, dec)
                parb[0].inc(sy.dma_start(out=wostg[:],
                                         in_=d_Wo[ecc * 128:(ecc + 1) * 128,
                                                  :]), 16)
            wo_ld.append(parb[0].n)

            @on("vector")
            def _(v, ecc=ecc, ld=wo_ld[ecc]):
                v.wait_ge(parb[0].h, ld)
                cs = slice(ecc * 1024, (ecc + 1) * 1024)
                v.tensor_copy(WoSh[:, cs], wostg[:])
                v.tensor_tensor(WoSl[:, cs], wostg[:], WoSh[:, cs],
                                AO.subtract)
                v.mark(sv)
            wo_dec.append(sv.n)
        wos_n = parb[1].n
        wos_dec_n = wo_dec[-1]

        @on("sync")
        def _(sy):
            saux.inc(sy.dma_start(out=bhgS[:], in_=d_bhg_t), 16)
            for d in range(8):
                pA.inc(sy.dma_start(
                    out=whg[0][:, d * 128:(d + 1) * 128],
                    in_=d_Wh[d * 128:(d + 1) * 128, HID:HID + 128]), 16)
        bhg_n = saux.n
        whg_lds = [pA.n]
        whg_dec = []

        def decompose_whg(ec):
            @on("vector")
            def _(v, ec=ec, ld=whg_lds[ec], fr=st_.n):
                v.wait_ge(par[ec % 2].h, ld)
                v.wait_ge(st_.h, fr)
                v.tensor_copy(whgh[ec % 2][:], whg[ec % 2][:])
                v.tensor_tensor(whgl[ec % 2][:], whg[ec % 2][:],
                                whgh[ec % 2][:], AO.subtract)
                v.mark(sv)
            whg_dec.append(sv.n)
        decompose_whg(0)

        c0_sbase = ss.n
        c0_gbase = sgd.n
        c0_jobs = []
        for ec in range(16):
            if ec + 1 < 16:
                used_n = st_.n

                @on("sync")
                def _(sy, ec=ec, used_n=used_n, dec=whg_dec[ec]):
                    if ec >= 1:
                        sy.wait_ge(st_.h, used_n)
                    sy.wait_ge(sv.h, dec)
                    for d in range(8):
                        par[(ec + 1) % 2].inc(sy.dma_start(
                            out=whg[(ec + 1) % 2][:, d * 128:(d + 1) * 128],
                            in_=d_Wh[d * 128:(d + 1) * 128,
                                     HID + (ec + 1) * 128:
                                     HID + (ec + 2) * 128]), 16)
                whg_lds.append(par[(ec + 1) % 2].n)
                decompose_whg(ec + 1)

            for ts4 in range(4):
                k = ec * 4 + ts4

                @on("tensor")
                def _(t_, ec=ec, ts4=ts4, dec=whg_dec[ec], k=k):
                    t_.wait_ge(sv.h, dec)
                    if k >= 2:
                        t_.wait_ge(ss.h, c0_sbase + k - 1)
                    bank = pb[ts4 % 2]
                    for d in range(8):
                        acs = slice(d * 2048 + ts4 * 512,
                                    d * 2048 + (ts4 + 1) * 512)
                        ws = slice(d * 128, (d + 1) * 128)
                        t_.matmul(bank[:], whgh[ec % 2][:, ws], NTh[:, acs],
                                  start=(d == 0), stop=False)
                        t_.matmul(bank[:], whgh[ec % 2][:, ws], NTl[:, acs],
                                  start=False, stop=False)
                        t_.matmul(bank[:], whgl[ec % 2][:, ws], NTh[:, acs],
                                  start=False, stop=(d == 7))
                    t_.mark(st_)
                c0_jobs.append((st_.n, k, ec))
                gev_n = c0_sbase + k + 1

                @on("gpsimd")
                def _(gp, ec=ec, ts4=ts4, gev_n=gev_n):
                    gp.wait_ge(ss.h, gev_n)
                    for half in range(2):
                        sgd.inc(gp.dma_start(
                            out=d_gateT[ts4 * 2 + half, ec],
                            in_=gte[ts4 % 2][:, half * 256:(half + 1) * 256]),
                            16)

        @on("scalar")
        def _(sc, jobs=list(c0_jobs)):
            for gmm_n, k, ec in jobs:
                sc.wait_ge(st_.h, gmm_n)
                if k == 0:
                    sc.wait_ge(saux.h, bhg_n)
                if k >= 1:
                    sc.wait_ge(sgd.h, c0_gbase + 32 * k)
                ss.inc(sc._e.activation(gte[k % 2][:], pb[k % 2][:],
                                        AF.Silu, bias=bhgS[:, k // 4:k // 4 + 1]))

        barrier()
        c0st.close()
        ntst.close()

        c1st = contextlib.ExitStack()
        qtg = [sb("qtg%d" % i, [128, 4096], st=c1st) for i in range(2)]
        gtg = [sb("gtg%d" % i, [128, 4096], st=c1st) for i in range(2)]
        ogs = sb("ogs", [128, 1024], st=c1st)
        ogh = [sb("ogh%d" % i, [128, 4096], BF16, st=c1st) for i in range(2)]
        ogl = [sb("ogl%d" % i, [128, 4096], BF16, st=c1st) for i in range(2)]
        xbc = [sb("xbc%d" % i, [128, DIM], st=c1st) for i in range(2)]
        xnbc = sb("xnbc", [128, DIM], st=c1st)
        outb = sb("outb", [128, DIM], st=c1st)

        def load_group_tiles(sy, gg, slot):
            for ec in range(16):
                par[slot].inc(sy.dma_start(
                    out=qtg[slot][:, ec * 256:(ec + 1) * 256],
                    in_=d_quadT[gg, ec]), 16)
                par[slot].inc(sy.dma_start(
                    out=gtg[slot][:, ec * 256:(ec + 1) * 256],
                    in_=d_gateT[gg, ec]), 16)

        @on("sync")
        def _(sy):
            load_group_tiles(sy, 0, 0)
            sxc.inc(sy.dma_start(out=xbc[0][:], in_=d_x[0:128, :]), 16)
            sxc.inc(sy.dma_start(out=xbc[1][:], in_=d_x[128:256, :]), 16)
        c1_lds = [pA.n]
        cx_lds = [sxc.n]
        wo_marks = {}
        ogev = {}

        def emit_lin_chunk(g, c, first):
            @on("tensor")
            def _(t_, g=g, c=c, first=first):
                if first:
                    t_.wait_ge(par[g % 2].h, c1_lds[g])
                    if g == 0:
                        t_.wait_ge(parb[1].h, wos_n)
                gate = ogev.get((g, c - 1), ogev.get((g - 1, 3), 0))
                t_.wait_ge(sv.h, gate)
                for k in range(4):
                    ec = 4 * c + k
                    t_.matmul(pb[k // 2][:, (k % 2) * 256:(k % 2) * 256 + 256],
                              lkv[:, ec * 128:(ec + 1) * 128],
                              linQF[:, g * 256:(g + 1) * 256],
                              start=True, stop=True)
                t_.mark(st_)
            lt_n = st_.n

            @on("vector")
            def _(v, g=g, c=c, lt_n=lt_n):
                v.wait_ge(st_.h, lt_n)
                for k in range(4):
                    ec = 4 * c + k
                    sl = slice(ec * 256, (ec + 1) * 256)
                    ssl = slice(k * 256, (k + 1) * 256)
                    v.tensor_tensor(ogs[:, ssl],
                                    pb[k // 2][:, (k % 2) * 256:
                                               (k % 2) * 256 + 256],
                                    qtg[g % 2][:, sl], AO.add)
                    v.tensor_tensor(ogs[:, ssl], ogs[:, ssl],
                                    gtg[g % 2][:, sl], AO.mult)
                sl4 = slice(4 * c * 256, 4 * (c + 1) * 256)
                v.tensor_copy(ogh[g % 2][:, sl4], ogs[:])
                v.tensor_tensor(ogl[g % 2][:, sl4], ogs[:],
                                ogh[g % 2][:, sl4], AO.subtract)
                v.mark(sv)
            ogev[(g, c)] = sv.n

        emit_lin_chunk(0, 0, True)
        emit_lin_chunk(0, 1, False)
        emit_lin_chunk(0, 2, False)
        emit_lin_chunk(0, 3, False)

        c1_sbase = ss.n
        c1_jobs = []
        for g in range(N_GROUPS):
            og_n = ogev[(g, 3)]
            if g + 1 < N_GROUPS:
                qg_done = ogev.get((g - 1, 3), 0)

                @on("sync")
                def _(sy, g=g, qg_done=qg_done):
                    sy.wait_ge(sv.h, qg_done)
                    load_group_tiles(sy, g + 1, (g + 1) % 2)
                c1_lds.append(par[(g + 1) % 2].n)

            @on("tensor")
            def _(t_, g=g, og_n=og_n, evac=wo_marks.get(g - 1, 0)):
                t_.wait_ge(sv.h, og_n if g else max(og_n, wos_dec_n))
                t_.wait_ge(sgp.h, evac)
                for ih in range(2):
                    for ds in range(2):
                        bank = pb[2 + ih * 2 + ds]
                        for ec in range(16):
                            acs = slice(ec * 256 + ih * 128,
                                        ec * 256 + ih * 128 + 128)
                            wcs = slice(ec * 1024 + ds * 512,
                                        ec * 1024 + ds * 512 + 512)
                            t_.matmul(bank[:], ogh[g % 2][:, acs],
                                      WoSh[:, wcs],
                                      start=(ec == 0), stop=False)
                            t_.matmul(bank[:], ogh[g % 2][:, acs],
                                      WoSl[:, wcs], start=False, stop=False)
                            t_.matmul(bank[:], ogl[g % 2][:, acs],
                                      WoSh[:, wcs], start=False,
                                      stop=(ec == 15))
                t_.mark(st_)
            wo_n = st_.n

            if g + 1 < N_GROUPS:
                for c in range(4):
                    emit_lin_chunk(g + 1, c, c == 0)

            for ih in range(2):
                t = 2 * g + ih
                tc = slice(t, t + 1)
                c1_jobs.append((wo_n, cx_lds[g], sgd.n, tc, ih))
                xn_n = c1_sbase + len(c1_jobs)

                @on("vector")
                def _(v, ih=ih, xn_n=xn_n):
                    v.wait_ge(ss.h, xn_n)
                    for ds in range(2):
                        sl = slice(ds * 512, (ds + 1) * 512)
                        v.tensor_tensor(outb[:, sl], pb[2 + ih * 2 + ds][:],
                                        bob[:, sl], AO.add)
                        v.tensor_tensor(outb[:, sl], outb[:, sl],
                                        xnbc[:, sl], AO.add)
                        v.tensor_tensor(outb[:, sl], outb[:, sl],
                                        xbc[ih][:, sl], AO.add)
                    v.mark(sv)
                ob_n = sv.n

                @on("gpsimd")
                def _(gp, t=t, ob_n=ob_n):
                    gp.wait_ge(sv.h, ob_n)
                    sgp.inc(gp.nop())
                    sgd.inc(gp.dma_start(out=d_out[t * 128:(t + 1) * 128, :],
                                         in_=outb[:], accum_op=AO.add), 16)
                wo_marks[g] = sgp.n

            if g + 1 < N_GROUPS:
                @on("sync")
                def _(sy, g=g):
                    sy.wait_ge(sgd.h, sgd.n)
                    sxc.inc(sy.dma_start(out=xbc[0][:],
                                         in_=d_x[(2 * g + 2) * 128:
                                                 (2 * g + 3) * 128, :]), 16)
                    sxc.inc(sy.dma_start(out=xbc[1][:],
                                         in_=d_x[(2 * g + 3) * 128:
                                                 (2 * g + 4) * 128, :]), 16)
                cx_lds.append(sxc.n)

        assert ss.n == c1_sbase, "unexpected ss incs inside C1 loop"

        @on("scalar")
        def _(sc, jobs=list(c1_jobs)):
            for wo_n, cx_n, sgd_b, tc, ih in jobs:
                sc.wait_ge(st_.h, wo_n)
                sc.wait_ge(sxc.h, cx_n)
                sc.wait_ge(sgd.h, sgd_b)
                ss.inc(sc._e.activation(xnbc[:], xbc[ih][:], AF.Identity,
                                        bias=nm0[:, tc], scale=rsq0[:, tc]))

        @on("gpsimd")
        def _(gp):
            gp.wait_ge(sgd.h, sgd.n)
            gp.wait_ge(sld.h, sld.n)

        barrier()
        c1st.close()
        c1wst.close()

    return nc


def _host_stage(inputs):
    import math
    f32 = np.float32
    x = np.asarray(inputs["L0_attn_x"], f32)
    b_h = np.asarray(inputs["L0_attn_b_h"], f32)
    rel = np.asarray(inputs["L0_attn_rel_emb"], f32)

    GSZ, NB, MAXD = 256, 32, 128
    pos = np.arange(GSZ)
    n = pos[:, None] - pos[None, :]
    ret = (n < 0).astype(np.int32) * (NB // 2)
    na = np.abs(n)
    me = (NB // 2) // 2
    small = na < me
    val = (np.log(np.maximum(na, 1).astype(f32) / me) /
           f32(math.log(MAXD / me)) * f32(NB // 2 - me)).astype(np.int32)
    val = np.minimum(me + val, NB // 2 - 1)
    bucket = ret + np.where(small, na, val)
    biasT = rel[bucket, 0].T
    biasTg = np.ascontiguousarray(
        np.concatenate([biasT[0:128, :], biasT[128:256, :]], axis=1), f32)

    shared = {
        "Wh": np.ascontiguousarray(np.asarray(inputs["L0_attn_W_h"], f32)),
        "Wqk": np.ascontiguousarray(np.asarray(inputs["L0_attn_W_qk"], f32)),
        "Wo": np.ascontiguousarray(np.asarray(inputs["L0_attn_W_o"], f32)),
        "W1T": np.ascontiguousarray(np.asarray(inputs["L0_ffn_w1"], f32).T),
        "W2T": np.ascontiguousarray(np.asarray(inputs["L0_ffn_w2"], f32).T),
        "bh_v": np.ascontiguousarray(b_h[:2048].reshape(1, -1)),
        "bhg_t": np.ascontiguousarray(b_h[2048:].reshape(16, 128).T),
        "bo_row": np.asarray(inputs["L0_attn_b_o"], f32).reshape(1, -1),
        "b1_row": np.asarray(inputs["L0_ffn_b1"], f32).reshape(1, -1),
        "b2_row": np.asarray(inputs["L0_ffn_b2"], f32).reshape(1, -1),
        "ln1g_row": np.asarray(inputs["L0_ffn_ln1_g"], f32).reshape(1, -1),
        "ln1b_row": np.asarray(inputs["L0_ffn_ln1_b"], f32).reshape(1, -1),
        "ln2g_row": np.asarray(inputs["L0_ffn_ln2_g"], f32).reshape(1, -1),
        "ln2b_row": np.asarray(inputs["L0_ffn_ln2_b"], f32).reshape(1, -1),
        "bqk_col": np.ascontiguousarray(
            np.asarray(inputs["L0_attn_b_qk"], f32).reshape(128, 1)),
        "gammaT": np.ascontiguousarray(
            np.asarray(inputs["L0_attn_qk_gamma"], f32).T),
        "betaT": np.ascontiguousarray(
            np.asarray(inputs["L0_attn_qk_beta"], f32).T),
        "normgT": np.ascontiguousarray(
            np.asarray(inputs["L0_attn_norm_g"], f32).reshape(8, 128).T),
        "normbT": np.ascontiguousarray(
            np.asarray(inputs["L0_attn_norm_b"], f32).reshape(8, 128).T),
        "biasTg": biasTg,
        "identf": np.eye(128, dtype=f32),
        "identb": np.eye(128, dtype=ml_dtypes.bfloat16),
        "ones_col": np.ones((128, 1), f32),
        "ones_row": np.ones((1, 128), f32),
    }
    in_maps = []
    for c in range(8):
        b, h = c // 2, c % 2
        im = dict(shared)
        im["x"] = np.ascontiguousarray(x[b, h * 2048:(h + 1) * 2048, :])
        in_maps.append(im)
    return in_maps


def kernel(**inputs):
    in_maps = _host_stage(inputs)
    nc = build(debug=False)
    res = run_bass_kernel_spmd(nc, in_maps, core_ids=list(range(8)))
    out = np.empty((4, 4096, 1024), np.float32)
    for c in range(8):
        b, h = c // 2, c % 2
        out[b, h * 2048:(h + 1) * 2048, :] = res.results[c]["out"]
    return out
